# version 16
# baseline (speedup 1.0000x reference)
import numpy as np
import ml_dtypes
import concourse.bass as bass
import concourse.mybir as mybir
from concourse.bass_utils import run_bass_kernel_spmd

F32 = mybir.dt.float32
BF16 = mybir.dt.bfloat16
AF = mybir.ActivationFunctionType
OP = mybir.AluOpType
AX = mybir.AxisListType

N, T, C, I = 4, 4096, 1024, 2816
NCH = C // 128
R = 512
SBW = 512
NSB = T // SBW
TO = T // 2
NTO = TO // 512
ICH = I // 128
EPS = 1e-5
RND = 12582912.0
SBUF_BASE = 16512
SBUF_BYTES = 212700

TAPS = ()
_CACHE = {}


class KB:

    NQ = 4

    def __init__(self, nc):
        self.nc = nc
        self.ops = {}
        self.sems = {}
        self.scnt = {}
        self.engine_sem = {}
        self.dma_idx = {}
        self.writers = {}
        self.readers = {}
        self.floor = {}
        for name in ("tensor", "vector", "scalar", "sync", "gpsimd"):
            sn = name + "_c"
            self.sems[sn] = nc.semaphore(sn).__enter__()
            self.scnt[sn] = 0
            self.engine_sem[name] = sn
            self.ops[name] = []
        for q in ("sync", "scalar", "gpsimd"):
            self.dma_idx[q] = 0
            for j in range(self.NQ):
                sn = f"{q}_d{j}"
                self.sems[sn] = nc.semaphore(sn).__enter__()
                self.scnt[sn] = 0
        self.sems["cc"] = nc.semaphore("cc").__enter__()
        self.scnt["cc"] = 0

    @staticmethod
    def _iv(ap):
        t = ap.tensor
        off = ap.offset
        if str(ap.space) == "MemorySpace.DRAM":
            dsz = mybir.dt.size(ap.dtype)
            return (t.name, 0, 1, off, off + max(1, ap.nbytes() // dsz))
        if str(ap.space) == "MemorySpace.PSUM":
            return (t.name, 0, 1 << 30, 0, 1 << 30)
        pstr = 1
        for d in t.shape[1:]:
            pstr *= d
        p0 = off // pstr
        f0 = off - p0 * pstr
        return (t.name, p0, p0 + ap.shape[0], f0, f0 + max(1, ap.free_size()))

    def set_floor(self):
        self.floor = {sn: v for sn, v in self.scnt.items() if v > 0}
        self.writers.clear()
        self.readers.clear()

    def emit(self, eng, fn, ins, outs, kind="compute"):
        deps = {}

        def add(tok):
            if tok is None:
                return
            e_, v_ = tok
            if v_ > deps.get(e_, 0):
                deps[e_] = v_
        for e_, v_ in self.floor.items():
            add((e_, v_))
        ivs_in = [self._iv(ap) for ap in ins]
        ivs_out = [self._iv(ap) for ap in outs]
        for nm, p0, p1, f0, f1 in ivs_in:
            for q0, q1, g0, g1, tok in self.writers.get(nm, ()):
                if q0 < p1 and p0 < q1 and g0 < f1 and f0 < g1:
                    add(tok)
        for nm, p0, p1, f0, f1 in ivs_out:
            for lst in (self.writers.get(nm), self.readers.get(nm)):
                if lst:
                    for q0, q1, g0, g1, tok in lst:
                        if q0 < p1 and p0 < q1 and g0 < f1 and f0 < g1:
                            add(tok)
        if kind == "dma":
            j = self.dma_idx[eng] % self.NQ
            self.dma_idx[eng] += 1
            sn = f"{eng}_d{j}"
            add((sn, self.scnt[sn]))
            self.scnt[sn] += 16
            inc_sem, inc = sn, 16
        elif kind == "cc":
            sn = "cc"
            self.scnt[sn] += 1
            inc_sem, inc = sn, 1
        else:
            sn = self.engine_sem[eng]
            self.scnt[sn] += 1
            inc_sem, inc = sn, 1
        self.ops[eng].append((fn, tuple(deps.items()), inc_sem, inc))
        tok = (inc_sem, self.scnt[inc_sem])
        for nm, p0, p1, f0, f1 in ivs_in:
            self.readers.setdefault(nm, []).append((p0, p1, f0, f1, tok))
        for nm, p0, p1, f0, f1 in ivs_out:
            cov = (lambda e, p0=p0, p1=p1, f0=f0, f1=f1:
                   p0 <= e[0] and e[1] <= p1 and f0 <= e[2] and e[3] <= f1)
            ws = self.writers.setdefault(nm, [])
            ws[:] = [e for e in ws if not cov(e)]
            ws.append((p0, p1, f0, f1, tok))
            rs = self.readers.get(nm)
            if rs:
                rs[:] = [e for e in rs if not cov(e)]
        return tok

    def finalize(self, block):
        final = {sn: v for sn, v in self.scnt.items() if v > 0}

        def make(eng_name):
            ops = self.ops[eng_name]
            sems = self.sems
            own = self.engine_sem[eng_name]

            def body(eng):
                waited = {}
                for fn, deps, inc_sem, inc in ops:
                    for de, dv in deps:
                        if de == own or waited.get(de, 0) >= dv:
                            continue
                        eng.wait_ge(sems[de], dv)
                        waited[de] = dv
                    fn(eng).then_inc(sems[inc_sem], inc)
                for de, dv in final.items():
                    if de == own or waited.get(de, 0) >= dv:
                        continue
                    eng.wait_ge(sems[de], dv)
            return body

        block.sync(make("sync"))
        block.vector(make("vector"))
        block.scalar(make("scalar"))
        block.tensor(make("tensor"))
        block.gpsimd(make("gpsimd"))


def _layout(specs):
    items = []
    for name, shape, dtype, p0, p1 in specs:
        nbytes = int(np.prod(shape[1:])) * mybir.dt.size(dtype)
        nbytes = (nbytes + 31) // 32 * 32
        items.append((nbytes, name, p0, p1))
    items.sort(key=lambda t: -t[0])
    placed = []
    offs = {}
    for nbytes, name, p0, p1 in items:
        cand = 0
        while True:
            conflict = None
            for off, sz, q0, q1 in placed:
                if cand < off + sz and off < cand + nbytes and p0 <= q1 and q0 <= p1:
                    conflict = off + sz
                    break
            if conflict is None:
                break
            cand = conflict
        assert cand + nbytes <= SBUF_BYTES, \
            f"SBUF layout overflow at {name}: {cand + nbytes}"
        placed.append((cand, nbytes, p0, p1))
        offs[name] = cand
    return offs


def build_nc():
    nc = bass.Bass("TRN2", target_bir_lowering=False)

    def inp(name, shape):
        return nc.dram_tensor(name, shape, F32, kind="ExternalInput")

    x_fm = inp("x_fm", [C, T])
    xo_fm = inp("xo_fm", [C, TO])
    wq_h = inp("wq", [C, C]); wk_h = inp("wk", [C, C]); wv_h = inp("wv", [C, R])
    lrw_h = inp("lrw", [C, 1]); outw_h = inp("outw", [C, C])
    w1_h = inp("w1", [C, C]); w2_h = inp("w2", [C, C])
    wg_h = inp("wgT", [C, I]); wu_h = inp("wuT", [C, I]); wd_h = inp("wdT", [I, C])
    ROWSPEC = [("gbq", C), ("gbk", C), ("gbv", R), ("outb", C), ("b1r", C),
               ("b2r", C), ("dng", C), ("dnb", C), ("mng", C), ("mnb", C),
               ("lrb", 1), ("clipx", 1), ("clipd", 1)]
    rows_in = {nm: inp(nm, [1, ln]) for nm, ln in ROWSPEC}
    out_h = nc.dram_tensor("out_fm", [C, TO], F32, kind="ExternalOutput")

    m_in_b = nc.dram_tensor("m_in_b", [C, R], BF16)
    m_out_b = nc.dram_tensor("m_out_b", [2 * C, R], BF16)
    gmax_in = nc.dram_tensor("gmax_in", [1, 16], F32)
    gmax_out = nc.dram_tensor("gmax_out", [8, 16], F32)
    hmax_in = nc.dram_tensor("hmax_in", [1, 16], F32)
    hmax_out = nc.dram_tensor("hmax_out", [8, 16], F32)
    h_spill = nc.dram_tensor("h_spill", [ICH, 128, TO], BF16)
    wgq_sp = nc.dram_tensor("wgq_sp", [C, I], BF16)
    row_sp = nc.dram_tensor("row_sp", [1, I], F32)
    wuq_sp = nc.dram_tensor("wuq_sp", [C, I], BF16)
    wdq_sp = nc.dram_tensor("wdq_sp", [I, C], BF16)

    eye_b_h = nc.inline_tensor(np.eye(128, dtype=ml_dtypes.bfloat16), name="c_eye_b")
    eye_f_h = nc.inline_tensor(np.eye(128, dtype=np.float32), name="c_eye_f")
    maskL_h = nc.inline_tensor(np.tril(np.ones((128, 128), np.float32), -1), name="c_maskL")
    maskU_h = nc.inline_tensor(np.triu(np.ones((128, 128), np.float32), 1), name="c_maskU")
    ones_f_h = nc.inline_tensor(np.ones((1, 512), np.float32), name="c_ones_f")
    ones_b_h = nc.inline_tensor(np.ones((1, 512), ml_dtypes.bfloat16), name="c_ones_b")
    onesc_b_h = nc.inline_tensor(np.ones((128, 1), ml_dtypes.bfloat16), name="c_onesc_b")
    epsc_h = nc.inline_tensor(
        np.broadcast_to(np.array([EPS, EPS * EPS], np.float32), (128, 2)).copy(),
        name="c_epsc")

    LSPEC = [
        ("onesr", [1, 512], F32, 0, 9), ("onesrb", [1, 512], BF16, 0, 9),
        ("onescb", [128, 1], BF16, 0, 9), ("epsc", [128, 2], F32, 0, 9),
        ("eyeb", [128, 128], BF16, 0, 9), ("eyef", [128, 128], F32, 0, 9),
        ("maskL_s", [128, 128], F32, 0, 1), ("maskU_s", [128, 128], F32, 0, 1),
        ("gbk_r", [1, C], BF16, 0, 9), ("gbv_r", [1, R], BF16, 0, 9),
        ("outb_r", [1, C], BF16, 0, 9), ("b2r_r", [1, C], BF16, 0, 9),
        ("lrb_r", [1, 1], BF16, 0, 9),
        ("clipx_r", [1, 1], F32, 0, 9), ("clipd_r", [1, 1], F32, 0, 9),
        ("dng_c", [128, NCH], F32, 0, 9), ("dnb_c", [128, NCH], F32, 0, 9),
        ("mng_c", [128, NCH], F32, 0, 9), ("mnb_c", [128, NCH], F32, 0, 9),
        ("gbq_c", [128, NCH], F32, 0, 9), ("b1r_c", [128, NCH], F32, 0, 9),
        ("gbkn_c", [128, NCH], F32, 0, 9),
        ("tr0", [1, 512], F32, 0, 9), ("tr1", [1, 512], F32, 0, 9),
        ("tr2", [1, 512], F32, 0, 9), ("tr3", [1, 512], F32, 0, 9),
        ("tro0", [1, 512], F32, 2, 2), ("tro1", [1, 512], F32, 2, 2),
        ("stt", [128, 512], F32, 0, 9), ("stt2", [128, 512], F32, 0, 9),
        ("sqf", [128, 512], BF16, 0, 9),
        ("lr_c", [128, T // 128], F32, 0, 9),
        ("mr1neg", [1, TO], F32, 0, 9),
        ("red0", [128, 1], F32, 0, 9), ("red1", [128, 1], F32, 0, 9),
        ("sval", [1, 16], F32, 0, 9), ("mx128", [1, 128], F32, 0, 9),
        ("gmaxa", [128, NCH * NTO], F32, 0, 9),
        ("habs", [128, ICH * NTO], F32, 0, 9),
        ("sxc", [128, 1], F32, 0, 9), ("sxv", [128, 1], F32, 0, 9),
        ("shc", [128, 1], F32, 0, 9), ("shv", [128, 1], F32, 0, 9),
        ("sgi_c", [128, ICH], F32, 0, 9), ("sui_c", [128, ICH], F32, 0, 9),
        ("sdh_c", [128, NCH], F32, 0, 9),
        ("wk_s", [128, NCH, C], BF16, 0, 1), ("wv_s", [128, NCH, R], BF16, 0, 1),
        ("lrw_s", [128, NCH, 1], BF16, 0, 1),
        ("xb0", [128, NCH, SBW], BF16, 0, 2), ("xb1", [128, NCH, SBW], BF16, 0, 1),
        ("ln1e", [128, NCH, SBW], BF16, 1, 2),
        ("ktm", [128, 4, C], BF16, 1, 1), ("kfm", [128, NCH, SBW], BF16, 1, 1),
        ("sqs", [128, C], BF16, 1, 1),
        ("rncol", [128, 4], F32, 1, 1), ("rnrow", [1, SBW], F32, 1, 1),
        ("u_tm", [128, 4, R], BF16, 1, 1), ("lru", [128, 4, R], BF16, 1, 1),
        ("S0", [128, NCH, R], BF16, 0, 1), ("S1", [128, NCH, R], BF16, 0, 1),
        ("M0", [128, NCH, R], BF16, 0, 1), ("M1", [128, NCH, R], BF16, 0, 1),
        ("gneg", [128, 6, 128], BF16, 1, 1),
        ("Lp_s", [128, 2, 128], F32, 1, 1), ("Rp_s", [128, 2, 128], F32, 1, 1),
        ("Pp_s", [128, 2, 128], F32, 1, 1), ("Qb_s", [128, 4, 128], BF16, 1, 1),
        ("rhsb", [128, 512], BF16, 1, 1),
        ("wq_s", [128, NCH, C], BF16, 2, 2),
        ("qfm", [128, NCH, TO], BF16, 2, 3),
        ("h1e", [128, NCH, TO], BF16, 2, 4),
        ("Mfull", [128, NCH, C], BF16, 3, 3),
        ("yfm", [128, NCH, TO], BF16, 3, 4),
        ("outw_s", [128, NCH, C], BF16, 4, 4),
        ("h2_s", [128, NCH, TO], BF16, 4, 5),
        ("w1_s", [128, NCH, C], BF16, 5, 5), ("w2_s", [128, NCH, C], BF16, 5, 5),
        ("ln2e", [128, NCH, 512], BF16, 5, 5), ("mid_s", [128, NCH, 512], BF16, 5, 5),
        ("x0t", [128, NCH, 512], F32, 5, 5),
        ("x1_s", [128, NCH, TO], BF16, 5, 9),
        ("g1t", [128, NCH, TO], BF16, 6, 6),
        ("xq_s", [128, NCH, TO], BF16, 6, 8),
        ("wsc", [128, I], BF16, 5, 7),
        ("wstr", [128, I], BF16, 5, 7),
        ("wabs", [128, I], BF16, 5, 7),
        ("wrow_g", [1, I], F32, 5, 9), ("wrow_u", [1, I], F32, 5, 9),
        ("wrow_d", [1, C], F32, 5, 9),
        ("invrow", [1, I], F32, 6, 9),
        ("wgt0", [128, NCH, 128], BF16, 8, 8), ("wgt1", [128, NCH, 128], BF16, 8, 8),
        ("wut0", [128, NCH, 128], BF16, 8, 8), ("wut1", [128, NCH, 128], BF16, 8, 8),
        ("h_t", [128, 512], BF16, 8, 9), ("silu_t", [128, 512], F32, 8, 9),
        ("up_t", [128, 512], F32, 8, 9),
        ("wdt0", [128, C], BF16, 9, 9), ("wdt1", [128, C], BF16, 9, 9),
        ("hq_t", [128, 512], BF16, 9, 9), ("outt", [128, 512], F32, 9, 9),
    ]
    offs = _layout(LSPEC)
    S_ = {}
    for name, shape, dtype, _, _ in LSPEC:
        S_[name] = nc.alloc_sbuf_tensor_at(name, shape, dtype,
                                           offset=SBUF_BASE + offs[name])

    g = S_
    onesr, onesrb, onescb = g["onesr"], g["onesrb"], g["onescb"]
    eyeb, eyef, maskL, maskU = g["eyeb"], g["eyef"], g["maskL_s"], g["maskU_s"]
    stt, stt2, sqf = g["stt"], g["stt2"], g["sqf"]
    tr0, tr1, tr2, tr3 = g["tr0"], g["tr1"], g["tr2"], g["tr3"]

    K = KB(nc)

    def dma(q, out, in_, slow=False):
        if slow:
            fn = lambda gg, o=out, i=in_: gg.dma_start(out=o, in_=i,
                                                       allow_slow_non_contiguous=True)
        else:
            fn = lambda gg, o=out, i=in_: gg.dma_start(out=o, in_=i)
        return K.emit(q, fn, [in_], [out], kind="dma")

    def mmg(bank_ap, specs):
        n = len(specs)
        tok = None
        for idx, (l, r) in enumerate(specs):
            tok = K.emit("tensor",
                         lambda gg, o=bank_ap, l=l, r=r, s=(idx == 0), p=(idx == n - 1):
                         gg.matmul(o, l, r, start=s, stop=p),
                         [l, r], [bank_ap])
        return tok

    def mma(bank_ap, spec, start, stop):
        (l, r) = spec
        return K.emit("tensor",
                      lambda gg, o=bank_ap, l=l, r=r, s=start, p=stop:
                      gg.matmul(o, l, r, start=s, stop=p),
                      [l, r], [bank_ap])

    def act(outap, inap, func, bias=0.0, scale=1.0):
        ins = [inap]
        if hasattr(bias, "tensor"):
            ins.append(bias)
        if hasattr(scale, "tensor"):
            ins.append(scale)
        return K.emit("scalar", lambda gg, o=outap, i=inap, f=func, b=bias, s=scale:
                      gg.activation(o, i, f, bias=b, scale=s), ins, [outap])

    def vtt(outap, a, b_, op):
        return K.emit("vector", lambda gg, o=outap, x=a, y=b_, p=op:
                      gg.tensor_tensor(o, x, y, p), [a, b_], [outap])

    def vts1(outap, a, s1, op):
        ins = [a] + ([s1] if hasattr(s1, "tensor") else [])
        return K.emit("vector", lambda gg, o=outap, x=a, u=s1, p=op:
                      gg.tensor_single_scalar(o, x, u, p), ins, [outap])

    def vts2(outap, a, s1, s2, op1, op2):
        return K.emit("vector", lambda gg, o=outap, x=a, u=s1, w=s2, p=op1, q=op2:
                      gg.tensor_scalar(o, x, u, w, p, q), [a], [outap])

    def vcopy(outap, inap):
        return K.emit("vector", lambda gg, o=outap, i=inap: gg.tensor_copy(o, i),
                      [inap], [outap])

    def vrec(outap, inap):
        return K.emit("vector", lambda gg, o=outap, i=inap: gg.reciprocal(o, i),
                      [inap], [outap])

    def vred(outap, inap, op, absval=False):
        return K.emit("vector", lambda gg, o=outap, i=inap, p=op, a=absval:
                      gg.tensor_reduce(o, i, AX.X, p, apply_absolute_value=a),
                      [inap], [outap])

    def vmemset(ap, val):
        return K.emit("vector", lambda gg, o=ap, v=val: gg.memset(o, v), [], [ap])

    def ccg(kind, op, groups, in_ap, out_ap):
        return K.emit("gpsimd", lambda gg, k=kind, o=op, r=groups, i=in_ap, u=out_ap:
                      gg.collective_compute(k, o, replica_groups=r, ins=[i], outs=[u]),
                      [in_ap], [out_ap], kind="cc")

    taps_out = []

    def tap(name, srcap, shape, dtype=F32):
        if name in TAPS:
            th = nc.dram_tensor("tap_" + name, shape, dtype, kind="ExternalOutput")
            dma("sync", th.ap(), srcap)
            taps_out.append(name)

    PB = [nc.alloc_psum_tensor(f"pb{i}", [128, 512], F32) for i in range(8)]

    for (nm, src) in [("wk_s", wk_h), ("wv_s", wv_h), ("lrw_s", lrw_h)]:
        dst = g[nm]
        for c in range(NCH):
            dma("gpsimd", dst[:, c, :], src.ap()[c * 128:(c + 1) * 128, :])
    for nm, ln in ROWSPEC:
        if nm in ("gbk", "gbv", "outb", "b2r", "lrb"):
            dma("gpsimd", g[nm + "_r"][:, :], rows_in[nm].ap())
        elif nm in ("clipx", "clipd"):
            dma("sync", g[nm + "_r"][:, :], rows_in[nm].ap())
    dma("sync", eyeb[:, :], eye_b_h.ap())
    dma("sync", eyef[:, :], eye_f_h.ap())
    dma("sync", maskL[:, :], maskL_h.ap())
    dma("sync", maskU[:, :], maskU_h.ap())
    dma("sync", onesr[:, :], ones_f_h.ap())
    dma("sync", onesrb[:, :], ones_b_h.ap())
    dma("sync", onescb[:, :], onesc_b_h.ap())
    dma("sync", g["epsc"][:, :], epsc_h.ap())
    for nm in ("dng", "dnb", "mng", "mnb", "gbq", "b1r"):
        dma("sync", g[nm + "_c"][:, :],
            rows_in[nm].ap()[0, :].rearrange("(a b) -> b a", b=128), slow=True)
    dma("sync", g["gbkn_c"][:, :],
        rows_in["gbk"].ap()[0, :].rearrange("(a b) -> b a", b=128), slow=True)
    vts1(g["gbkn_c"][:, :], g["gbkn_c"][:, :], -1.0, OP.mult)
    vmemset(g["S0"][:, :, :], 0.0)
    vmemset(g["M0"][:, :, :], 0.0)

    def ln_stats(xc, s_out, ms_out, fused, r1_out=None, mr1_out=None):
        mmg(PB[6][0:1, :], [(onescb[:, 0:1], xc(c)) for c in range(NCH)])
        for c in range(NCH):
            act(sqf[:, :], xc(c), AF.Square)
            mma(PB[7][0:1, :], (onescb[:, 0:1], sqf[:, :]), c == 0, c == NCH - 1)
        vts1(tr0[:, :], PB[6][0:1, :], 1.0 / C, OP.mult)
        vts1(tr1[:, :], PB[7][0:1, :], 1.0 / C, OP.mult)
        act(tr2[:, :], tr0[:, :], AF.Square)
        vtt(tr1[:, :], tr1[:, :], tr2[:, :], OP.subtract)
        if fused:
            act(tr2[:, :], tr1[:, :], AF.Ln, bias=g["epsc"][0:1, 1:2], scale=1.0 + EPS)
        else:
            act(tr2[:, :], tr1[:, :], AF.Ln, bias=g["epsc"][0:1, 0:1])
        act(tr2[:, :], tr2[:, :], AF.Exp, scale=-0.5)
        vcopy(s_out, tr2[:, :])
        vtt(ms_out, tr0[:, :], tr2[:, :], OP.mult)
        if r1_out is not None:
            act(r1_out, tr1[:, :], AF.Ln, bias=g["epsc"][0:1, 0:1])
            act(r1_out, r1_out, AF.Exp, scale=-0.5)
            vtt(mr1_out, tr0[:, :], r1_out, OP.mult)

    Scur, Snew = g["S0"], g["S1"]
    Mcur, Mnew = g["M0"], g["M1"]
    pair_idx = {}
    pi = 0
    for i_ in range(4):
        for j_ in range(i_):
            pair_idx[(j_, i_)] = pi
            pi += 1
    ln1e, ktm, kfm, sqs = g["ln1e"], g["ktm"], g["kfm"], g["sqs"]
    rncol, rnrow = g["rncol"], g["rnrow"]
    u_tm, lru, gneg = g["u_tm"], g["lru"], g["gneg"]
    Lp, Rp, Pp, Qb, rhsb = g["Lp_s"], g["Rp_s"], g["Pp_s"], g["Qb_s"], g["rhsb"]
    wk, wv, lrw = g["wk_s"], g["wv_s"], g["lrw_s"]
    lr_c = g["lr_c"]

    for s in range(NSB):
        xb = g["xb0"] if s % 2 == 0 else g["xb1"]
        for c in range(NCH):
            dma("gpsimd", xb[:, c, :],
                x_fm.ap()[c * 128:(c + 1) * 128, s * SBW:(s + 1) * SBW])

        def xc(c, xb=xb):
            return xb[:, c, :]

        ln_stats(xc, tr2[:, :], tr3[:, :], fused=True)
        mmg(PB[6][:, :], [(onesr[0:1, 0:128], tr2[0:1, :])])
        mmg(PB[7][:, :], [(onesr[0:1, 0:128], tr3[0:1, :])])
        for c in range(NCH):
            vtt(stt[:, :], xc(c), PB[6][:, :], OP.mult)
            vtt(stt2[:, :], stt[:, :], PB[7][:, :], OP.subtract)
            act(ln1e[:, c, :], stt2[:, :], AF.Identity,
                bias=g["dnb_c"][:, c:c + 1], scale=g["dng_c"][:, c:c + 1])

        for tc in range(4):
            gtc = s * 4 + tc
            specs = [(ln1e[:, c, tc * 128:(tc + 1) * 128], lrw[:, c, 0:1])
                     for c in range(NCH)]
            specs.append((onesrb[0:1, 0:128], g["lrb_r"][0:1, 0:1]))
            mmg(PB[1][:, 0:1], specs)
            act(lr_c[:, gtc:gtc + 1], PB[1][:, 0:1], AF.Identity)

        for tc in range(4):
            for hf in range(2):
                bank = PB[6] if hf == 0 else PB[7]
                specs = [(ln1e[:, c, tc * 128:(tc + 1) * 128],
                          wk[:, c, hf * 512:(hf + 1) * 512]) for c in range(NCH)]
                specs.append((onesrb[0:1, 0:128],
                              g["gbk_r"][0:1, hf * 512:(hf + 1) * 512]))
                mmg(bank[:, :], specs)
                act(ktm[:, tc, hf * 512:(hf + 1) * 512], bank[:, :], AF.Identity)
            vtt(sqs[:, :], ktm[:, tc, :], ktm[:, tc, :], OP.mult)
            vred(g["red0"][:, :], sqs[:, :], OP.add)
            act(g["red0"][:, :], g["red0"][:, :], AF.Ln)
            act(rncol[:, tc:tc + 1], g["red0"][:, :], AF.Exp, scale=-0.5)
            vts1(ktm[:, tc, :], ktm[:, tc, :], rncol[:, tc:tc + 1], OP.mult)

        for ck in range(NCH):
            bank = PB[6] if ck % 2 == 0 else PB[7]
            mmg(bank[:, :], [(wk[:, c, ck * 128:(ck + 1) * 128], ln1e[:, c, :])
                             for c in range(NCH)])
            act(kfm[:, ck, :], bank[:, :], AF.Identity,
                bias=g["gbkn_c"][:, ck:ck + 1], scale=-1.0)
        for ck in range(NCH):
            act(sqf[:, :], kfm[:, ck, :], AF.Square)
            mma(PB[1][0:1, :], (onescb[:, 0:1], sqf[:, :]), ck == 0, ck == NCH - 1)
        act(tr0[:, :], PB[1][0:1, :], AF.Ln)
        act(rnrow[:, :], tr0[:, :], AF.Exp, scale=-0.5)
        mmg(PB[0][:, :], [(onesr[0:1, 0:128], rnrow[0:1, :])])
        for ck in range(NCH):
            vtt(kfm[:, ck, :], kfm[:, ck, :], PB[0][:, :], OP.mult)

        for i_ in range(4):
            for j_ in range(i_ + 1):
                mmg(PB[2][:, 0:128],
                    [(kfm[:, ck, j_ * 128:(j_ + 1) * 128],
                      kfm[:, ck, i_ * 128:(i_ + 1) * 128]) for ck in range(NCH)])
                if j_ < i_:
                    act(gneg[:, pair_idx[(j_, i_)], :], PB[2][:, 0:128],
                        AF.Identity, scale=-1.0)
                else:
                    vtt(Lp[:, 0, :], PB[2][:, 0:128], maskL[:, :], OP.mult)
                    vtt(Rp[:, 0, :], PB[2][:, 0:128], maskU[:, :], OP.mult)
                    vtt(Pp[:, 0, :], eyef[:, :], Rp[:, 0, :], OP.subtract)
                    cl = cr = cp = 0
                    for mlev in range(1, 6):
                        mmg(PB[3][:, 0:128], [(Rp[:, cr, :], Lp[:, cl, :])])
                        mmg(PB[3][:, 128:256], [(Lp[:, cl, :], Rp[:, cr, :])])
                        act(Lp[:, 1 - cl, :], PB[3][:, 0:128], AF.Identity)
                        act(Rp[:, 1 - cr, :], PB[3][:, 128:256], AF.Identity)
                        mmg(PB[3][:, 256:384],
                            [(Lp[:, 1 - cl, :], Pp[:, cp, :]),
                             (eyef[:, :], Pp[:, cp, :])])
                        act(Pp[:, 1 - cp, :], PB[3][:, 256:384], AF.Identity)
                        cl, cr, cp = 1 - cl, 1 - cr, 1 - cp
                    act(Qb[:, i_, :], Pp[:, cp, :], AF.Identity)

            specs = [(ln1e[:, c, i_ * 128:(i_ + 1) * 128], wv[:, c, :])
                     for c in range(NCH)]
            specs.append((onesrb[0:1, 0:128], g["gbv_r"][0:1, :]))
            specs += [(kfm[:, ck, i_ * 128:(i_ + 1) * 128], Scur[:, ck, :])
                      for ck in range(NCH)]
            specs += [(gneg[:, pair_idx[(j_, i_)], :], u_tm[:, j_, :])
                      for j_ in range(i_)]
            mmg(PB[0][:, :], specs)
            vcopy(rhsb[:, :], PB[0][:, :])
            mmg(PB[1][:, :], [(Qb[:, i_, :], rhsb[:, :])])
            act(u_tm[:, i_, :], PB[1][:, :], AF.Identity)
            act(lru[:, i_, :], PB[1][:, :], AF.Identity,
                scale=lr_c[:, (s * 4 + i_):(s * 4 + i_) + 1])

        for ck in range(NCH):
            bank = PB[4] if ck % 2 == 0 else PB[5]
            specs = [(eyeb[:, :], Scur[:, ck, :])]
            specs += [(ktm[:, tc, ck * 128:(ck + 1) * 128], u_tm[:, tc, :])
                      for tc in range(4)]
            mmg(bank[:, :], specs)
            act(Snew[:, ck, :], bank[:, :], AF.Identity)
        for ck in range(NCH):
            bank = PB[4] if ck % 2 == 0 else PB[5]
            specs = [(eyeb[:, :], Mcur[:, ck, :])]
            specs += [(ktm[:, tc, ck * 128:(ck + 1) * 128], lru[:, tc, :])
                      for tc in range(4)]
            mmg(bank[:, :], specs)
            act(Mnew[:, ck, :], bank[:, :], AF.Identity)
        Scur, Snew = Snew, Scur
        Mcur, Mnew = Mnew, Mcur

        if s == 0:
            tap("ln1e0", ln1e[:, :, :], [128, NCH, SBW], BF16)
            tap("ktm0", ktm[:, :, :], [128, 4, C], BF16)
            tap("kfm0", kfm[:, :, :], [128, NCH, SBW], BF16)
            tap("u0", u_tm[:, :, :], [128, 4, R], BF16)
            tap("Q0", Qb[:, :, :], [128, 4, 128], BF16)
            tap("lr0", lr_c[:, 0:4], [128, 4], F32)
        if s == NSB - 1:
            tap("St", Scur[:, :, :], [128, NCH, R], BF16)
            tap("Mt", Mcur[:, :, :], [128, NCH, R], BF16)

    for ck in range(NCH):
        dma("sync", m_in_b.ap()[ck * 128:(ck + 1) * 128, :], Mcur[:, ck, :])
    ccg("AllGather", OP.bypass, [[0, 1], [2, 3], [4, 5], [6, 7]],
        m_in_b.ap(), m_out_b.ap())

    K.set_floor()

    wqs, qfm, h1e = g["wq_s"], g["qfm"], g["h1e"]
    tro0, tro1 = g["tro0"], g["tro1"]
    for c in range(NCH):
        dma("gpsimd", wqs[:, c, :], wq_h.ap()[c * 128:(c + 1) * 128, :])
    for t_ in range(NTO):
        xb = g["xb0"]
        for c in range(NCH):
            dma("gpsimd", xb[:, c, :],
                xo_fm.ap()[c * 128:(c + 1) * 128, t_ * 512:(t_ + 1) * 512])

        def xoc(c, xb=xb):
            return xb[:, c, :]

        sl = slice(t_ * 512, (t_ + 1) * 512)
        ln_stats(xoc, tr2[:, :], tr3[:, :], fused=True,
                 r1_out=tro0[:, :], mr1_out=tro1[:, :])
        mmg(PB[6][:, :], [(onesr[0:1, 0:128], tr2[0:1, :])])
        mmg(PB[7][:, :], [(onesr[0:1, 0:128], tr3[0:1, :])])
        mmg(PB[0][:, :], [(onesr[0:1, 0:128], tro0[0:1, :])])
        for c in range(NCH):
            vtt(stt[:, :], xoc(c), PB[6][:, :], OP.mult)
            vtt(stt2[:, :], stt[:, :], PB[7][:, :], OP.subtract)
            act(ln1e[:, c, :], stt2[:, :], AF.Identity,
                bias=g["dnb_c"][:, c:c + 1], scale=g["dng_c"][:, c:c + 1])
            vtt(h1e[:, c, sl], xoc(c), PB[0][:, :], OP.mult)
        for ck in range(NCH):
            bank = PB[1] if ck % 2 == 0 else PB[2]
            mmg(bank[:, :], [(wqs[:, c, ck * 128:(ck + 1) * 128], ln1e[:, c, :])
                             for c in range(NCH)])
            act(qfm[:, ck, sl], bank[:, :], AF.Identity,
                bias=g["gbq_c"][:, ck:ck + 1])
        vts1(g["mr1neg"][0:1, sl], tro1[0:1, :], -1.0, OP.mult)
    tap("qfmt", qfm[:, :, :], [128, NCH, TO], BF16)
    tap("h1et", h1e[:, :, :], [128, NCH, TO], BF16)

    K.set_floor()

    Mfull, yfm = g["Mfull"], g["yfm"]
    for ck in range(NCH):
        dma("sync", Mfull[:, ck, 0:512], m_out_b.ap()[ck * 128:(ck + 1) * 128, :])
        dma("sync", Mfull[:, ck, 512:1024],
            m_out_b.ap()[C + ck * 128:C + (ck + 1) * 128, :])
    tap("Mfull", Mfull[:, :, :], [128, NCH, C], BF16)
    for t_ in range(NTO):
        sl = slice(t_ * 512, (t_ + 1) * 512)
        for cv in range(NCH):
            bank = PB[3] if cv % 2 == 0 else PB[4]
            mmg(bank[:, :], [(Mfull[:, ck, cv * 128:(cv + 1) * 128], qfm[:, ck, sl])
                             for ck in range(NCH)])
            act(yfm[:, cv, sl], bank[:, :], AF.Identity)
    tap("yfm", yfm[:, :, :], [128, NCH, TO], BF16)

    K.set_floor()

    outw, h2 = g["outw_s"], g["h2_s"]
    for c in range(NCH):
        dma("gpsimd", outw[:, c, :], outw_h.ap()[c * 128:(c + 1) * 128, :])
    for t_ in range(NTO):
        sl = slice(t_ * 512, (t_ + 1) * 512)
        for co in range(NCH):
            bank = PB[0] if co % 2 == 0 else PB[1]
            specs = [(outw[:, cv, co * 128:(co + 1) * 128], yfm[:, cv, sl])
                     for cv in range(NCH)]
            specs.append((g["outb_r"][0:1, co * 128:(co + 1) * 128], onesrb[0:1, :]))
            specs.append((onesr[0:1, 0:128], g["mr1neg"][0:1, sl]))
            mmg(bank[:, :], specs)
            vtt(h2[:, co, sl], h1e[:, co, sl], bank[:, :], OP.add)
    tap("h2t", h2[:, :, :], [128, NCH, TO], BF16)

    K.set_floor()

    w1s, w2s, ln2e, mid, x0t, x1 = (g["w1_s"], g["w2_s"], g["ln2e"], g["mid_s"],
                                    g["x0t"], g["x1_s"])
    for c in range(NCH):
        dma("gpsimd", w1s[:, c, :], w1_h.ap()[c * 128:(c + 1) * 128, :])
        dma("gpsimd", w2s[:, c, :], w2_h.ap()[c * 128:(c + 1) * 128, :])
    for t_ in range(NTO):
        sl = slice(t_ * 512, (t_ + 1) * 512)

        def h2c(c, sl=sl):
            return h2[:, c, sl]

        ln_stats(h2c, tr2[:, :], tr3[:, :], fused=False)
        mmg(PB[6][:, :], [(onesr[0:1, 0:128], tr2[0:1, :])])
        mmg(PB[7][:, :], [(onesr[0:1, 0:128], tr3[0:1, :])])
        for c in range(NCH):
            vtt(stt[:, :], h2[:, c, sl], PB[6][:, :], OP.mult)
            vtt(stt2[:, :], stt[:, :], PB[7][:, :], OP.subtract)
            act(ln2e[:, c, :], stt2[:, :], AF.Identity,
                bias=g["mnb_c"][:, c:c + 1], scale=g["mng_c"][:, c:c + 1])
        for co in range(NCH):
            bank = PB[0] if co % 2 == 0 else PB[1]
            mmg(bank[:, :], [(w1s[:, c, co * 128:(co + 1) * 128], ln2e[:, c, :])
                             for c in range(NCH)])
            act(mid[:, co, :], bank[:, :], AF.Relu, bias=g["b1r_c"][:, co:co + 1])
        for c in range(NCH):
            dma("sync", x0t[:, c, :],
                xo_fm.ap()[c * 128:(c + 1) * 128, t_ * 512:(t_ + 1) * 512])
        for co in range(NCH):
            bank = PB[2] if co % 2 == 0 else PB[3]
            specs = [(w2s[:, c, co * 128:(co + 1) * 128], mid[:, c, :])
                     for c in range(NCH)]
            specs.append((g["b2r_r"][0:1, co * 128:(co + 1) * 128], onesrb[0:1, :]))
            mmg(bank[:, :], specs)
            vtt(stt[:, :], x0t[:, co, :], h2[:, co, sl], OP.add)
            vtt(x1[:, co, sl], stt[:, :], bank[:, :], OP.add)
    tap("x1t", x1[:, :, :], [128, NCH, TO], BF16)

    wsc, wabs2 = g["wsc"], [g["wabs"], g["wabs"]]
    wstr2 = [g["wstr"], g["wstr"]]

    def wq_pass1(src_h, nchunks, width, wrow):
        for c in range(nchunks):
            wstr = wstr2[c % 2]
            wabs = wabs2[c % 2]
            dma("gpsimd", wstr[:, 0:width], src_h.ap()[c * 128:(c + 1) * 128, :])
            act(wabs[:, 0:width], wstr[:, 0:width], AF.Abs)
            if c == 0:
                vcopy(wsc[:, 0:width], wabs[:, 0:width])
            else:
                vtt(wsc[:, 0:width], wsc[:, 0:width], wabs[:, 0:width], OP.max)
        h = 64
        while h >= 1:
            for w0 in range(0, width, 512):
                w1_ = min(width, w0 + 512)
                dma("sync", sqf[0:h, 0:w1_ - w0], wsc[h:2 * h, w0:w1_])
                vtt(wsc[0:h, w0:w1_], wsc[0:h, w0:w1_], sqf[0:h, 0:w1_ - w0],
                    OP.max)
            h //= 2
        vcopy(wrow[0:1, 0:width], wsc[0:1, 0:width])

    wq_pass1(wg_h, NCH, I, g["wrow_g"])
    wq_pass1(wu_h, NCH, I, g["wrow_u"])
    wq_pass1(wd_h, ICH, C, g["wrow_d"])

    K.set_floor()

    g1t, xqs = g["g1t"], g["xq_s"]
    for t_ in range(NTO):
        sl = slice(t_ * 512, (t_ + 1) * 512)

        def x1c(c, sl=sl):
            return x1[:, c, sl]

        ln_stats(x1c, tr2[:, :], tr3[:, :], fused=False)
        mmg(PB[6][:, :], [(onesr[0:1, 0:128], tr2[0:1, :])])
        mmg(PB[7][:, :], [(onesr[0:1, 0:128], tr3[0:1, :])])
        for c in range(NCH):
            vtt(stt[:, :], x1[:, c, sl], PB[6][:, :], OP.mult)
            vtt(g1t[:, c, sl], stt[:, :], PB[7][:, :], OP.subtract)
            vred(g["gmaxa"][:, t_ * NCH + c:t_ * NCH + c + 1], g1t[:, c, sl],
                 OP.max, absval=True)
    tap("g1tt", g1t[:, :, :], [128, NCH, TO], BF16)

    def partition_tree_max():
        h = 64
        while h >= 1:
            dma("sync", g["red1"][0:h, 0:1], g["red0"][h:2 * h, 0:1])
            vtt(g["red0"][0:h, 0:1], g["red0"][0:h, 0:1], g["red1"][0:h, 0:1],
                OP.max)
            h //= 2

    vred(g["red0"][:, :], g["gmaxa"][:, :], OP.max)
    partition_tree_max()
    mmg(PB[2][0:1, 0:16], [(g["red0"][0:1, 0:1], onesr[0:1, 0:16])])
    act(g["sval"][:, :], PB[2][0:1, 0:16], AF.Identity)
    dma("sync", gmax_in.ap(), g["sval"][:, :])
    invrow = g["invrow"]

    def wq_pass2(src_h, spill_h, nchunks, width, wrow):
        act(invrow[0:1, 0:width], wrow[0:1, 0:width], AF.Ln, scale=1.0 / 127.0)
        act(invrow[0:1, 0:width], invrow[0:1, 0:width], AF.Exp, scale=-1.0)
        for c in range(nchunks):
            wstr = wstr2[c % 2]
            dma("gpsimd", wstr[:, 0:width], src_h.ap()[c * 128:(c + 1) * 128, :])
            for w0 in range(0, width, 512):
                w1_ = min(width, w0 + 512)
                mmg(PB[2][:, 0:w1_ - w0],
                    [(onesr[0:1, 0:128], invrow[0:1, w0:w1_])])
                vtt(stt[:, 0:w1_ - w0], wstr[:, w0:w1_], PB[2][:, 0:w1_ - w0],
                    OP.mult)
                vts2(wstr[:, w0:w1_], stt[:, 0:w1_ - w0], RND, RND,
                     OP.add, OP.subtract)
            dma("sync", spill_h.ap()[c * 128:(c + 1) * 128, :], wstr[:, 0:width])

    ccg("AllGather", OP.bypass, [[0, 1, 2, 3, 4, 5, 6, 7]],
        gmax_in.ap(), gmax_out.ap())

    wq_pass2(wg_h, wgq_sp, NCH, I, g["wrow_g"])
    wq_pass2(wu_h, wuq_sp, NCH, I, g["wrow_u"])
    wq_pass2(wd_h, wdq_sp, ICH, C, g["wrow_d"])
    for (wr, colt, width) in ((g["wrow_g"], g["sgi_c"], I),
                              (g["wrow_u"], g["sui_c"], I),
                              (g["wrow_d"], g["sdh_c"], C)):
        vts1(invrow[0:1, 0:width], wr[0:1, :], 1.0 / 127.0, OP.mult)
        dma("sync", row_sp.ap()[0:1, 0:width], invrow[0:1, 0:width])
        dma("sync", colt[:, :],
            row_sp.ap()[0, 0:width].rearrange("(a b) -> b a", b=128), slow=True)

    dma("sync", g["mx128"][:, :], gmax_out.ap().rearrange("a b -> (a b)").unsqueeze(0))
    vred(g["sval"][0:1, 0:1], g["mx128"][:, :], OP.max)
    act(g["sval"][0:1, 1:2], g["clipx_r"][0:1, 0:1], AF.Sigmoid)
    vtt(g["sval"][0:1, 2:3], g["sval"][0:1, 0:1], g["sval"][0:1, 1:2], OP.mult)
    vts1(g["sval"][0:1, 2:3], g["sval"][0:1, 2:3], 1.0 / 127.0, OP.mult)
    vrec(g["sval"][0:1, 3:4], g["sval"][0:1, 2:3])
    mmg(PB[2][:, 0:1], [(onesr[0:1, 0:128], g["sval"][0:1, 3:4])])
    act(g["sxc"][:, :], PB[2][:, 0:1], AF.Identity)
    mmg(PB[2][:, 0:1], [(onesr[0:1, 0:128], g["sval"][0:1, 2:3])])
    act(g["sxv"][:, :], PB[2][:, 0:1], AF.Identity)
    for t_ in range(NTO):
        sl = slice(t_ * 512, (t_ + 1) * 512)
        for c in range(NCH):
            vts1(stt[:, :], g1t[:, c, sl], g["sxc"][:, 0:1], OP.mult)
            vts2(stt[:, :], stt[:, :], RND, RND, OP.add, OP.subtract)
            vts2(xqs[:, c, sl], stt[:, :], -128.0, 127.0, OP.max, OP.min)
    tap("xqt", xqs[:, :, :], [128, NCH, TO], BF16)
    vts1(g["sgi_c"][:, :], g["sgi_c"][:, :], g["sxv"][:, 0:1], OP.mult)
    vts1(g["sui_c"][:, :], g["sui_c"][:, :], g["sxv"][:, 0:1], OP.mult)

    K.set_floor()

    wgt = [g["wgt0"], g["wgt1"]]
    wut = [g["wut0"], g["wut1"]]
    h_t, silu_t, up_t = g["h_t"], g["silu_t"], g["up_t"]
    for ic in range(ICH):
        bw = wgt[ic % 2]
        bu = wut[ic % 2]
        for hh in range(NCH):
            dma("sync", bw[:, hh, :],
                wgq_sp.ap()[hh * 128:(hh + 1) * 128, ic * 128:(ic + 1) * 128])
            dma("sync", bu[:, hh, :],
                wuq_sp.ap()[hh * 128:(hh + 1) * 128, ic * 128:(ic + 1) * 128])
        for t_ in range(NTO):
            sl = slice(t_ * 512, (t_ + 1) * 512)
            bg = PB[0] if t_ % 2 == 0 else PB[1]
            bu_ = PB[2] if t_ % 2 == 0 else PB[3]
            mmg(bg[:, :], [(bw[:, hh, :], xqs[:, hh, sl]) for hh in range(NCH)])
            mmg(bu_[:, :], [(bu[:, hh, :], xqs[:, hh, sl]) for hh in range(NCH)])
            act(silu_t[:, :], bg[:, :], AF.Silu, scale=g["sgi_c"][:, ic:ic + 1])
            act(up_t[:, :], bu_[:, :], AF.Identity, scale=g["sui_c"][:, ic:ic + 1])
            vtt(h_t[:, :], silu_t[:, :], up_t[:, :], OP.mult)
            vred(g["habs"][:, t_ * ICH + ic:t_ * ICH + ic + 1], h_t[:, :],
                 OP.max, absval=True)
            dma("sync", h_spill.ap()[ic, :, t_ * 512:(t_ + 1) * 512], h_t[:, :])

    vred(g["red0"][:, :], g["habs"][:, :], OP.max)
    partition_tree_max()
    mmg(PB[2][0:1, 0:16], [(g["red0"][0:1, 0:1], onesr[0:1, 0:16])])
    act(g["sval"][:, :], PB[2][0:1, 0:16], AF.Identity)
    dma("sync", hmax_in.ap(), g["sval"][:, :])
    ccg("AllGather", OP.bypass, [[0, 1, 2, 3, 4, 5, 6, 7]],
        hmax_in.ap(), hmax_out.ap())
    dma("sync", g["mx128"][:, :], hmax_out.ap().rearrange("a b -> (a b)").unsqueeze(0))
    vred(g["sval"][0:1, 0:1], g["mx128"][:, :], OP.max)
    act(g["sval"][0:1, 1:2], g["clipd_r"][0:1, 0:1], AF.Sigmoid)
    vtt(g["sval"][0:1, 2:3], g["sval"][0:1, 0:1], g["sval"][0:1, 1:2], OP.mult)
    vts1(g["sval"][0:1, 2:3], g["sval"][0:1, 2:3], 1.0 / 127.0, OP.mult)
    vrec(g["sval"][0:1, 3:4], g["sval"][0:1, 2:3])
    mmg(PB[2][:, 0:1], [(onesr[0:1, 0:128], g["sval"][0:1, 3:4])])
    act(g["shc"][:, :], PB[2][:, 0:1], AF.Identity)
    mmg(PB[2][:, 0:1], [(onesr[0:1, 0:128], g["sval"][0:1, 2:3])])
    act(g["shv"][:, :], PB[2][:, 0:1], AF.Identity)
    vts1(g["sdh_c"][:, :], g["sdh_c"][:, :], g["shv"][:, 0:1], OP.mult)

    K.set_floor()

    wdt = [g["wdt0"], g["wdt1"]]
    hq_t, outt = g["hq_t"], g["outt"]
    for t_ in range(NTO):
        sl = slice(t_ * 512, (t_ + 1) * 512)
        for ic in range(ICH):
            wb = wdt[ic % 2]
            dma("sync", wb[:, :], wdq_sp.ap()[ic * 128:(ic + 1) * 128, :])
            dma("sync", h_t[:, :], h_spill.ap()[ic, :, t_ * 512:(t_ + 1) * 512])
            vts1(stt[:, :], h_t[:, :], g["shc"][:, 0:1], OP.mult)
            vts2(stt[:, :], stt[:, :], RND, RND, OP.add, OP.subtract)
            vts2(hq_t[:, :], stt[:, :], -128.0, 127.0, OP.max, OP.min)
            for ho in range(NCH):
                mma(PB[ho][:, :], (wb[:, ho * 128:(ho + 1) * 128], hq_t[:, :]),
                    ic == 0, ic == ICH - 1)
        for ho in range(NCH):
            act(outt[:, :], PB[ho][:, :], AF.Identity, scale=g["sdh_c"][:, ho:ho + 1])
            vtt(outt[:, :], outt[:, :], x1[:, ho, sl], OP.add)
            dma("sync", out_h.ap()[ho * 128:(ho + 1) * 128, t_ * 512:(t_ + 1) * 512],
                outt[:, :])

    with nc.Block() as block:
        K.finalize(block)
    return nc, taps_out


def kernel(L0_attn_x, L0_attn_dn_g, L0_attn_dn_b, L0_attn_gates_w, L0_attn_gates_b,
           L0_attn_lr_w, L0_attn_lr_b, L0_attn_out_w, L0_attn_out_b, L0_attn_mn_g,
           L0_attn_mn_b, L0_attn_w1, L0_attn_b1, L0_attn_w2, L0_attn_b2,
           L0_ffn_clip_x, L0_ffn_clip_down, L0_ffn_w_gate, L0_ffn_w_up,
           L0_ffn_w_down):
    x = np.asarray(L0_attn_x, np.float32)
    gw = np.ascontiguousarray(np.asarray(L0_attn_gates_w, np.float32))
    gb = np.asarray(L0_attn_gates_b, np.float32)
    shared = {
        "wq": np.ascontiguousarray(gw[:, 0:C]),
        "wk": np.ascontiguousarray(gw[:, C:2 * C]),
        "lrw": np.ascontiguousarray(np.asarray(L0_attn_lr_w, np.float32)),
        "outw": np.ascontiguousarray(np.asarray(L0_attn_out_w, np.float32)),
        "w1": np.ascontiguousarray(np.asarray(L0_attn_w1, np.float32)),
        "w2": np.ascontiguousarray(np.asarray(L0_attn_w2, np.float32)),
        "wgT": np.ascontiguousarray(np.asarray(L0_ffn_w_gate, np.float32).T),
        "wuT": np.ascontiguousarray(np.asarray(L0_ffn_w_up, np.float32).T),
        "wdT": np.ascontiguousarray(np.asarray(L0_ffn_w_down, np.float32).T),
        "gbq": gb[0:C].reshape(1, C),
        "gbk": gb[C:2 * C].reshape(1, C),
        "outb": np.asarray(L0_attn_out_b, np.float32).reshape(1, C),
        "b1r": np.asarray(L0_attn_b1, np.float32).reshape(1, C),
        "b2r": np.asarray(L0_attn_b2, np.float32).reshape(1, C),
        "dng": np.asarray(L0_attn_dn_g, np.float32).reshape(1, C),
        "dnb": np.asarray(L0_attn_dn_b, np.float32).reshape(1, C),
        "mng": np.asarray(L0_attn_mn_g, np.float32).reshape(1, C),
        "mnb": np.asarray(L0_attn_mn_b, np.float32).reshape(1, C),
        "lrb": np.asarray(L0_attn_lr_b, np.float32).reshape(1, 1),
        "clipx": np.asarray(L0_ffn_clip_x, np.float32).reshape(1, 1),
        "clipd": np.asarray(L0_ffn_clip_down, np.float32).reshape(1, 1),
    }
    shared = {k: np.ascontiguousarray(v) for k, v in shared.items()}

    if "nc" not in _CACHE:
        _CACHE["nc"] = build_nc()
    nc, tap_names = _CACHE["nc"]

    in_maps = []
    for core in range(8):
        b, hh = core // 2, core % 2
        xb_fm = np.ascontiguousarray(x[b].T)
        xo = np.ascontiguousarray(xb_fm[:, hh * TO:(hh + 1) * TO])
        m = dict(shared)
        m["x_fm"] = xb_fm
        m["xo_fm"] = xo
        m["wv"] = np.ascontiguousarray(gw[:, 2 * C + hh * R: 2 * C + (hh + 1) * R])
        m["gbv"] = np.ascontiguousarray(
            gb[2 * C + hh * R: 2 * C + (hh + 1) * R].reshape(1, R))
        in_maps.append(m)

    res = run_bass_kernel_spmd(nc, in_maps, core_ids=list(range(8)))
    out = np.empty((N, T, C), np.float32)
    for core in range(8):
        b, hh = core // 2, core % 2
        out[b, hh * TO:(hh + 1) * TO, :] = res.results[core]["out_fm"].T
    if TAPS:
        kernel.taps = [{n: r["tap_" + n] for n in tap_names} for r in res.results]
    return out


# version 17
# speedup vs baseline: 1.0886x; 1.0886x over previous
import numpy as np
import ml_dtypes
import concourse.bass as bass
import concourse.mybir as mybir
from concourse.bass_utils import run_bass_kernel_spmd

F32 = mybir.dt.float32
BF16 = mybir.dt.bfloat16
AF = mybir.ActivationFunctionType
OP = mybir.AluOpType
AX = mybir.AxisListType

N, T, C, I = 4, 4096, 1024, 2816
NCH = C // 128
R = 512
SBW = 512
NSB = T // SBW
TO = T // 2
NTO = TO // 512
ICH = I // 128
EPS = 1e-5
RND = 12582912.0
SBUF_BASE = 16512
SBUF_BYTES = 212700

TAPS = ()
_CACHE = {}


class KB:

    NQ = 8

    def __init__(self, nc):
        self.nc = nc
        self.ops = {}
        self.sems = {}
        self.scnt = {}
        self.engine_sem = {}
        self.dma_idx = {}
        self.writers = {}
        self.readers = {}
        self.floor = {}
        for name in ("tensor", "vector", "scalar", "sync", "gpsimd"):
            sn = name + "_c"
            self.sems[sn] = nc.semaphore(sn).__enter__()
            self.scnt[sn] = 0
            self.engine_sem[name] = sn
            self.ops[name] = []
        for q in ("sync", "scalar", "gpsimd"):
            self.dma_idx[q] = 0
            for j in range(self.NQ):
                sn = f"{q}_d{j}"
                self.sems[sn] = nc.semaphore(sn).__enter__()
                self.scnt[sn] = 0
        self.sems["cc"] = nc.semaphore("cc").__enter__()
        self.scnt["cc"] = 0

    @staticmethod
    def _iv(ap):
        t = ap.tensor
        off = ap.offset
        if str(ap.space) == "MemorySpace.DRAM":
            dsz = mybir.dt.size(ap.dtype)
            return (t.name, 0, 1, off, off + max(1, ap.nbytes() // dsz))
        if str(ap.space) == "MemorySpace.PSUM":
            return (t.name, 0, 1 << 30, 0, 1 << 30)
        pstr = 1
        for d in t.shape[1:]:
            pstr *= d
        p0 = off // pstr
        f0 = off - p0 * pstr
        return (t.name, p0, p0 + ap.shape[0], f0, f0 + max(1, ap.free_size()))

    def set_floor(self):
        self.floor = {sn: v for sn, v in self.scnt.items() if v > 0}
        self.writers.clear()
        self.readers.clear()

    def emit(self, eng, fn, ins, outs, kind="compute"):
        deps = {}

        def add(tok):
            if tok is None:
                return
            e_, v_ = tok
            if v_ > deps.get(e_, 0):
                deps[e_] = v_
        for e_, v_ in self.floor.items():
            add((e_, v_))
        ivs_in = [self._iv(ap) for ap in ins]
        ivs_out = [self._iv(ap) for ap in outs]
        for nm, p0, p1, f0, f1 in ivs_in:
            for q0, q1, g0, g1, tok in self.writers.get(nm, ()):
                if q0 < p1 and p0 < q1 and g0 < f1 and f0 < g1:
                    add(tok)
        for nm, p0, p1, f0, f1 in ivs_out:
            for lst in (self.writers.get(nm), self.readers.get(nm)):
                if lst:
                    for q0, q1, g0, g1, tok in lst:
                        if q0 < p1 and p0 < q1 and g0 < f1 and f0 < g1:
                            add(tok)
        if kind == "dma":
            j = self.dma_idx[eng] % self.NQ
            self.dma_idx[eng] += 1
            sn = f"{eng}_d{j}"
            add((sn, self.scnt[sn]))
            self.scnt[sn] += 16
            inc_sem, inc = sn, 16
        elif kind == "cc":
            sn = "cc"
            self.scnt[sn] += 1
            inc_sem, inc = sn, 1
        else:
            sn = self.engine_sem[eng]
            self.scnt[sn] += 1
            inc_sem, inc = sn, 1
        self.ops[eng].append((fn, tuple(deps.items()), inc_sem, inc))
        tok = (inc_sem, self.scnt[inc_sem])
        for nm, p0, p1, f0, f1 in ivs_in:
            self.readers.setdefault(nm, []).append((p0, p1, f0, f1, tok))
        for nm, p0, p1, f0, f1 in ivs_out:
            cov = (lambda e, p0=p0, p1=p1, f0=f0, f1=f1:
                   p0 <= e[0] and e[1] <= p1 and f0 <= e[2] and e[3] <= f1)
            ws = self.writers.setdefault(nm, [])
            ws[:] = [e for e in ws if not cov(e)]
            ws.append((p0, p1, f0, f1, tok))
            rs = self.readers.get(nm)
            if rs:
                rs[:] = [e for e in rs if not cov(e)]
        return tok

    def finalize(self, block):
        final = {sn: v for sn, v in self.scnt.items() if v > 0}

        def make(eng_name):
            ops = self.ops[eng_name]
            sems = self.sems
            own = self.engine_sem[eng_name]

            def body(eng):
                waited = {}
                for fn, deps, inc_sem, inc in ops:
                    for de, dv in deps:
                        if de == own or waited.get(de, 0) >= dv:
                            continue
                        eng.wait_ge(sems[de], dv)
                        waited[de] = dv
                    fn(eng).then_inc(sems[inc_sem], inc)
                for de, dv in final.items():
                    if de == own or waited.get(de, 0) >= dv:
                        continue
                    eng.wait_ge(sems[de], dv)
            return body

        block.sync(make("sync"))
        block.vector(make("vector"))
        block.scalar(make("scalar"))
        block.tensor(make("tensor"))
        block.gpsimd(make("gpsimd"))


def _layout(specs):
    items = []
    for name, shape, dtype, p0, p1 in specs:
        nbytes = int(np.prod(shape[1:])) * mybir.dt.size(dtype)
        nbytes = (nbytes + 31) // 32 * 32
        items.append((nbytes, name, p0, p1))
    items.sort(key=lambda t: -t[0])
    placed = []
    offs = {}
    for nbytes, name, p0, p1 in items:
        cand = 0
        while True:
            conflict = None
            for off, sz, q0, q1 in placed:
                if cand < off + sz and off < cand + nbytes and p0 <= q1 and q0 <= p1:
                    conflict = off + sz
                    break
            if conflict is None:
                break
            cand = conflict
        assert cand + nbytes <= SBUF_BYTES, \
            f"SBUF layout overflow at {name}: {cand + nbytes}"
        placed.append((cand, nbytes, p0, p1))
        offs[name] = cand
    return offs


def build_nc():
    nc = bass.Bass("TRN2", target_bir_lowering=False)

    def inp(name, shape):
        return nc.dram_tensor(name, shape, F32, kind="ExternalInput")

    x_fm = inp("x_fm", [C, T])
    xo_fm = inp("xo_fm", [C, TO])
    wq_h = inp("wq", [C, C]); wk_h = inp("wk", [C, C]); wv_h = inp("wv", [C, R])
    lrw_h = inp("lrw", [C, 1]); outw_h = inp("outw", [C, C])
    w1_h = inp("w1", [C, C]); w2_h = inp("w2", [C, C])
    wg_h = inp("wgT", [C, I]); wu_h = inp("wuT", [C, I]); wd_h = inp("wdT", [I, C])
    ROWSPEC = [("gbq", C), ("gbk", C), ("gbv", R), ("outb", C), ("b1r", C),
               ("b2r", C), ("dng", C), ("dnb", C), ("mng", C), ("mnb", C),
               ("lrb", 1), ("clipx", 1), ("clipd", 1)]
    rows_in = {nm: inp(nm, [1, ln]) for nm, ln in ROWSPEC}
    out_h = nc.dram_tensor("out_fm", [C, TO], F32, kind="ExternalOutput")

    m_in_b = nc.dram_tensor("m_in_b", [C, R], BF16)
    m_out_b = nc.dram_tensor("m_out_b", [2 * C, R], BF16)
    gmax_in = nc.dram_tensor("gmax_in", [1, 16], F32)
    gmax_out = nc.dram_tensor("gmax_out", [8, 16], F32)
    hmax_in = nc.dram_tensor("hmax_in", [1, 16], F32)
    hmax_out = nc.dram_tensor("hmax_out", [8, 16], F32)
    h_spill = nc.dram_tensor("h_spill", [ICH, 128, TO], BF16)
    wgq_sp = nc.dram_tensor("wgq_sp", [C, I], BF16)
    row_sp = nc.dram_tensor("row_sp", [1, I], F32)
    wuq_sp = nc.dram_tensor("wuq_sp", [C, I], BF16)
    wdq_sp = nc.dram_tensor("wdq_sp", [I, C], BF16)

    eye_b_h = nc.inline_tensor(np.eye(128, dtype=ml_dtypes.bfloat16), name="c_eye_b")
    eye_f_h = nc.inline_tensor(np.eye(128, dtype=np.float32), name="c_eye_f")
    maskL_h = nc.inline_tensor(np.tril(np.ones((128, 128), np.float32), -1), name="c_maskL")
    maskU_h = nc.inline_tensor(np.triu(np.ones((128, 128), np.float32), 1), name="c_maskU")
    ones_f_h = nc.inline_tensor(np.ones((1, 512), np.float32), name="c_ones_f")
    ones_b_h = nc.inline_tensor(np.ones((1, 512), ml_dtypes.bfloat16), name="c_ones_b")
    onesc_b_h = nc.inline_tensor(np.ones((128, 1), ml_dtypes.bfloat16), name="c_onesc_b")
    epsc_h = nc.inline_tensor(
        np.broadcast_to(np.array([EPS, EPS * EPS], np.float32), (128, 2)).copy(),
        name="c_epsc")

    LSPEC = [
        ("onesr", [1, 512], F32, 0, 9), ("onesrb", [1, 512], BF16, 0, 9),
        ("onescb", [128, 1], BF16, 0, 9), ("epsc", [128, 2], F32, 0, 9),
        ("eyeb", [128, 128], BF16, 0, 9), ("eyef", [128, 128], F32, 0, 9),
        ("maskL_s", [128, 128], F32, 0, 9), ("maskU_s", [128, 128], F32, 0, 9),
        ("gbk_r", [1, C], BF16, 0, 9), ("gbv_r", [1, R], BF16, 0, 9),
        ("outb_r", [1, C], BF16, 0, 9), ("b2r_r", [1, C], BF16, 0, 9),
        ("lrb_r", [1, 1], BF16, 0, 9),
        ("clipx_r", [1, 1], F32, 0, 9), ("clipd_r", [1, 1], F32, 0, 9),
        ("dng_c", [128, NCH], F32, 0, 9), ("dnb_c", [128, NCH], F32, 0, 9),
        ("mng_c", [128, NCH], F32, 0, 9), ("mnb_c", [128, NCH], F32, 0, 9),
        ("gbq_c", [128, NCH], F32, 0, 9), ("b1r_c", [128, NCH], F32, 0, 9),
        ("gbkn_c", [128, NCH], F32, 0, 9),
        ("tr0", [1, 512], F32, 0, 9), ("tr1", [1, 512], F32, 0, 9),
        ("tr2", [1, 512], F32, 0, 9), ("tr3", [1, 512], F32, 0, 9),
        ("tro0", [1, 512], F32, 0, 9), ("tro1", [1, 512], F32, 0, 9),
        ("stt", [128, 512], F32, 0, 9), ("stt2", [128, 512], F32, 0, 9),
        ("sqf", [128, 512], BF16, 0, 9),
        ("lr_c", [128, T // 128], F32, 0, 9),
        ("mr1neg", [1, TO], F32, 0, 9),
        ("red0", [128, 1], F32, 0, 9), ("red1", [128, 1], F32, 0, 9),
        ("sval", [1, 16], F32, 0, 9), ("mx128", [1, 128], F32, 0, 9),
        ("gmaxa", [128, NCH * NTO], F32, 0, 9),
        ("habs", [128, ICH * NTO], F32, 0, 9),
        ("sxc", [128, 1], F32, 0, 9), ("sxv", [128, 1], F32, 0, 9),
        ("shc", [128, 1], F32, 0, 9), ("shv", [128, 1], F32, 0, 9),
        ("sgi_c", [128, ICH], F32, 0, 9), ("sui_c", [128, ICH], F32, 0, 9),
        ("sdh_c", [128, NCH], F32, 0, 9),
        ("wk_s", [128, NCH, C], BF16, 0, 1), ("wv_s", [128, NCH, R], BF16, 0, 1),
        ("lrw_s", [128, NCH, 1], BF16, 0, 1),
        ("xb0", [128, NCH, SBW], BF16, 0, 2), ("xb1", [128, NCH, SBW], BF16, 0, 1),
        ("ln1e", [128, NCH, SBW], BF16, 1, 2),
        ("ktm", [128, 4, C], BF16, 1, 1), ("kfm", [128, NCH, SBW], BF16, 1, 1),
        ("sqs", [128, C], BF16, 1, 1),
        ("rncol", [128, 4], F32, 1, 1), ("rnrow", [1, SBW], F32, 1, 1),
        ("u_tm", [128, 4, R], BF16, 1, 1), ("lru", [128, 4, R], BF16, 1, 1),
        ("S0", [128, NCH, R], BF16, 0, 1), ("S1", [128, NCH, R], BF16, 0, 1),
        ("M0", [128, NCH, R], BF16, 0, 1), ("M1", [128, NCH, R], BF16, 0, 1),
        ("gneg", [128, 6, 128], BF16, 1, 1),
        ("Lp_s", [128, 2, 128], F32, 1, 1), ("Rp_s", [128, 2, 128], F32, 1, 1),
        ("Pp_s", [128, 2, 128], F32, 1, 1), ("Qb_s", [128, 4, 128], BF16, 1, 1),
        ("rhsb", [128, 512], BF16, 1, 1),
        ("wq_s", [128, NCH, C], BF16, 2, 2),
        ("qfm", [128, NCH, TO], BF16, 2, 3),
        ("h1e", [128, NCH, TO], BF16, 2, 4),
        ("Mfull", [128, NCH, C], BF16, 3, 3),
        ("yfm", [128, NCH, TO], BF16, 3, 4),
        ("outw_s", [128, NCH, C], BF16, 4, 4),
        ("h2_s", [128, NCH, TO], BF16, 4, 5),
        ("w1_s", [128, NCH, C], BF16, 5, 5), ("w2_s", [128, NCH, C], BF16, 5, 5),
        ("ln2e", [128, NCH, 512], BF16, 5, 5), ("mid_s", [128, NCH, 512], BF16, 5, 5),
        ("x0t", [128, NCH, 512], F32, 5, 5),
        ("x1_s", [128, NCH, TO], BF16, 5, 9),
        ("g1t", [128, NCH, TO], BF16, 6, 6),
        ("xq_s", [128, NCH, TO], BF16, 6, 8),
        ("wsc", [128, I], F32, 6, 7),
        ("wstr", [128, I], BF16, 6, 7),
        ("wstr2", [128, I], BF16, 6, 7),
        ("wabs", [128, I], BF16, 6, 7),
        ("wabs2", [128, I], BF16, 6, 7),
        ("wrow_g", [1, I], F32, 6, 9), ("wrow_u", [1, I], F32, 6, 9),
        ("wrow_d", [1, C], F32, 6, 9),
        ("invrow", [1, I], F32, 6, 9),
        ("wgt0", [128, NCH, 128], BF16, 8, 8), ("wgt1", [128, NCH, 128], BF16, 8, 8),
        ("wut0", [128, NCH, 128], BF16, 8, 8), ("wut1", [128, NCH, 128], BF16, 8, 8),
        ("h_t", [128, 512], BF16, 8, 9), ("silu_t", [128, 512], F32, 8, 9),
        ("up_t", [128, 512], F32, 8, 9),
        ("wdt0", [128, C], BF16, 9, 9), ("wdt1", [128, C], BF16, 9, 9),
        ("hq_t", [128, 512], BF16, 9, 9), ("outt", [128, 512], F32, 9, 9),
    ]
    offs = _layout(LSPEC)
    S_ = {}
    for name, shape, dtype, _, _ in LSPEC:
        S_[name] = nc.alloc_sbuf_tensor_at(name, shape, dtype,
                                           offset=SBUF_BASE + offs[name])

    g = S_
    onesr, onesrb, onescb = g["onesr"], g["onesrb"], g["onescb"]
    eyeb, eyef, maskL, maskU = g["eyeb"], g["eyef"], g["maskL_s"], g["maskU_s"]
    stt, stt2, sqf = g["stt"], g["stt2"], g["sqf"]
    tr0, tr1, tr2, tr3 = g["tr0"], g["tr1"], g["tr2"], g["tr3"]

    K = KB(nc)

    def dma(q, out, in_, slow=False):
        if slow:
            fn = lambda gg, o=out, i=in_: gg.dma_start(out=o, in_=i,
                                                       allow_slow_non_contiguous=True)
        else:
            fn = lambda gg, o=out, i=in_: gg.dma_start(out=o, in_=i)
        return K.emit(q, fn, [in_], [out], kind="dma")

    def mmg(bank_ap, specs):
        n = len(specs)
        tok = None
        for idx, (l, r) in enumerate(specs):
            tok = K.emit("tensor",
                         lambda gg, o=bank_ap, l=l, r=r, s=(idx == 0), p=(idx == n - 1):
                         gg.matmul(o, l, r, start=s, stop=p),
                         [l, r], [bank_ap])
        return tok

    def mma(bank_ap, spec, start, stop):
        (l, r) = spec
        return K.emit("tensor",
                      lambda gg, o=bank_ap, l=l, r=r, s=start, p=stop:
                      gg.matmul(o, l, r, start=s, stop=p),
                      [l, r], [bank_ap])

    def act(outap, inap, func, bias=0.0, scale=1.0):
        ins = [inap]
        if hasattr(bias, "tensor"):
            ins.append(bias)
        if hasattr(scale, "tensor"):
            ins.append(scale)
        return K.emit("scalar", lambda gg, o=outap, i=inap, f=func, b=bias, s=scale:
                      gg.activation(o, i, f, bias=b, scale=s), ins, [outap])

    def vtt(outap, a, b_, op):
        return K.emit("vector", lambda gg, o=outap, x=a, y=b_, p=op:
                      gg.tensor_tensor(o, x, y, p), [a, b_], [outap])

    def vts1(outap, a, s1, op):
        ins = [a] + ([s1] if hasattr(s1, "tensor") else [])
        return K.emit("vector", lambda gg, o=outap, x=a, u=s1, p=op:
                      gg.tensor_single_scalar(o, x, u, p), ins, [outap])

    def vts2(outap, a, s1, s2, op1, op2):
        return K.emit("vector", lambda gg, o=outap, x=a, u=s1, w=s2, p=op1, q=op2:
                      gg.tensor_scalar(o, x, u, w, p, q), [a], [outap])

    def vcopy(outap, inap):
        return K.emit("vector", lambda gg, o=outap, i=inap: gg.tensor_copy(o, i),
                      [inap], [outap])

    def vrec(outap, inap):
        return K.emit("vector", lambda gg, o=outap, i=inap: gg.reciprocal(o, i),
                      [inap], [outap])

    def vred(outap, inap, op, absval=False):
        return K.emit("vector", lambda gg, o=outap, i=inap, p=op, a=absval:
                      gg.tensor_reduce(o, i, AX.X, p, apply_absolute_value=a),
                      [inap], [outap])

    def vmemset(ap, val):
        return K.emit("vector", lambda gg, o=ap, v=val: gg.memset(o, v), [], [ap])

    def ccg(kind, op, groups, in_ap, out_ap):
        return K.emit("gpsimd", lambda gg, k=kind, o=op, r=groups, i=in_ap, u=out_ap:
                      gg.collective_compute(k, o, replica_groups=r, ins=[i], outs=[u]),
                      [in_ap], [out_ap], kind="cc")

    taps_out = []

    def tap(name, srcap, shape, dtype=F32):
        if name in TAPS:
            th = nc.dram_tensor("tap_" + name, shape, dtype, kind="ExternalOutput")
            dma("sync", th.ap(), srcap)
            taps_out.append(name)

    PB = [nc.alloc_psum_tensor(f"pb{i}", [128, 512], F32) for i in range(8)]

    for (nm, src) in [("wk_s", wk_h), ("wv_s", wv_h), ("lrw_s", lrw_h)]:
        dst = g[nm]
        for c in range(NCH):
            dma("gpsimd", dst[:, c, :], src.ap()[c * 128:(c + 1) * 128, :])
    for nm, ln in ROWSPEC:
        if nm in ("gbk", "gbv", "outb", "b2r", "lrb"):
            dma("gpsimd", g[nm + "_r"][:, :], rows_in[nm].ap())
        elif nm in ("clipx", "clipd"):
            dma("sync", g[nm + "_r"][:, :], rows_in[nm].ap())
    dma("sync", eyeb[:, :], eye_b_h.ap())
    dma("sync", eyef[:, :], eye_f_h.ap())
    dma("sync", maskL[:, :], maskL_h.ap())
    dma("sync", maskU[:, :], maskU_h.ap())
    dma("sync", onesr[:, :], ones_f_h.ap())
    dma("sync", onesrb[:, :], ones_b_h.ap())
    dma("sync", onescb[:, :], onesc_b_h.ap())
    dma("sync", g["epsc"][:, :], epsc_h.ap())
    for nm in ("dng", "dnb", "mng", "mnb", "gbq", "b1r"):
        dma("sync", g[nm + "_c"][:, :],
            rows_in[nm].ap()[0, :].rearrange("(a b) -> b a", b=128), slow=True)
    dma("sync", g["gbkn_c"][:, :],
        rows_in["gbk"].ap()[0, :].rearrange("(a b) -> b a", b=128), slow=True)
    vts1(g["gbkn_c"][:, :], g["gbkn_c"][:, :], -1.0, OP.mult)
    vmemset(g["S0"][:, :, :], 0.0)
    vmemset(g["M0"][:, :, :], 0.0)

    def ln_stats(xc, s_out, ms_out, fused, r1_out=None, mr1_out=None):
        mmg(PB[6][0:1, :], [(onescb[:, 0:1], xc(c)) for c in range(NCH)])
        for c in range(NCH):
            act(sqf[:, :], xc(c), AF.Square)
            mma(PB[7][0:1, :], (onescb[:, 0:1], sqf[:, :]), c == 0, c == NCH - 1)
        vts1(tr0[:, :], PB[6][0:1, :], 1.0 / C, OP.mult)
        vts1(tr1[:, :], PB[7][0:1, :], 1.0 / C, OP.mult)
        act(tr2[:, :], tr0[:, :], AF.Square)
        vtt(tr1[:, :], tr1[:, :], tr2[:, :], OP.subtract)
        if fused:
            act(tr2[:, :], tr1[:, :], AF.Ln, bias=g["epsc"][0:1, 1:2], scale=1.0 + EPS)
        else:
            act(tr2[:, :], tr1[:, :], AF.Ln, bias=g["epsc"][0:1, 0:1])
        act(tr2[:, :], tr2[:, :], AF.Exp, scale=-0.5)
        vcopy(s_out, tr2[:, :])
        vtt(ms_out, tr0[:, :], tr2[:, :], OP.mult)
        if r1_out is not None:
            act(r1_out, tr1[:, :], AF.Ln, bias=g["epsc"][0:1, 0:1])
            act(r1_out, r1_out, AF.Exp, scale=-0.5)
            vtt(mr1_out, tr0[:, :], r1_out, OP.mult)

    Scur, Snew = g["S0"], g["S1"]
    Mcur, Mnew = g["M0"], g["M1"]
    pair_idx = {}
    pi = 0
    for i_ in range(4):
        for j_ in range(i_):
            pair_idx[(j_, i_)] = pi
            pi += 1
    ln1e, ktm, kfm, sqs = g["ln1e"], g["ktm"], g["kfm"], g["sqs"]
    rncol, rnrow = g["rncol"], g["rnrow"]
    u_tm, lru, gneg = g["u_tm"], g["lru"], g["gneg"]
    Lp, Rp, Pp, Qb, rhsb = g["Lp_s"], g["Rp_s"], g["Pp_s"], g["Qb_s"], g["rhsb"]
    wk, wv, lrw = g["wk_s"], g["wv_s"], g["lrw_s"]
    lr_c = g["lr_c"]

    for s in range(NSB):
        xb = g["xb0"] if s % 2 == 0 else g["xb1"]
        for c in range(NCH):
            dma("gpsimd", xb[:, c, :],
                x_fm.ap()[c * 128:(c + 1) * 128, s * SBW:(s + 1) * SBW])

        def xc(c, xb=xb):
            return xb[:, c, :]

        ln_stats(xc, tr2[:, :], tr3[:, :], fused=True)
        mmg(PB[6][:, :], [(onesr[0:1, 0:128], tr2[0:1, :])])
        mmg(PB[7][:, :], [(onesr[0:1, 0:128], tr3[0:1, :])])
        for c in range(NCH):
            vtt(stt[:, :], xc(c), PB[6][:, :], OP.mult)
            vtt(stt2[:, :], stt[:, :], PB[7][:, :], OP.subtract)
            act(ln1e[:, c, :], stt2[:, :], AF.Identity,
                bias=g["dnb_c"][:, c:c + 1], scale=g["dng_c"][:, c:c + 1])

        for tc in range(4):
            gtc = s * 4 + tc
            specs = [(ln1e[:, c, tc * 128:(tc + 1) * 128], lrw[:, c, 0:1])
                     for c in range(NCH)]
            specs.append((onesrb[0:1, 0:128], g["lrb_r"][0:1, 0:1]))
            mmg(PB[1][:, 0:1], specs)
            act(lr_c[:, gtc:gtc + 1], PB[1][:, 0:1], AF.Identity)

        for tc in range(4):
            for hf in range(2):
                bank = PB[6] if hf == 0 else PB[7]
                specs = [(ln1e[:, c, tc * 128:(tc + 1) * 128],
                          wk[:, c, hf * 512:(hf + 1) * 512]) for c in range(NCH)]
                specs.append((onesrb[0:1, 0:128],
                              g["gbk_r"][0:1, hf * 512:(hf + 1) * 512]))
                mmg(bank[:, :], specs)
                act(ktm[:, tc, hf * 512:(hf + 1) * 512], bank[:, :], AF.Identity)
            vtt(sqs[:, :], ktm[:, tc, :], ktm[:, tc, :], OP.mult)
            vred(g["red0"][:, :], sqs[:, :], OP.add)
            act(g["red0"][:, :], g["red0"][:, :], AF.Ln)
            act(rncol[:, tc:tc + 1], g["red0"][:, :], AF.Exp, scale=-0.5)
            vts1(ktm[:, tc, :], ktm[:, tc, :], rncol[:, tc:tc + 1], OP.mult)

        for ck in range(NCH):
            bank = PB[6] if ck % 2 == 0 else PB[7]
            mmg(bank[:, :], [(wk[:, c, ck * 128:(ck + 1) * 128], ln1e[:, c, :])
                             for c in range(NCH)])
            act(kfm[:, ck, :], bank[:, :], AF.Identity,
                bias=g["gbkn_c"][:, ck:ck + 1], scale=-1.0)
        for ck in range(NCH):
            act(sqf[:, :], kfm[:, ck, :], AF.Square)
            mma(PB[1][0:1, :], (onescb[:, 0:1], sqf[:, :]), ck == 0, ck == NCH - 1)
        act(tr0[:, :], PB[1][0:1, :], AF.Ln)
        act(rnrow[:, :], tr0[:, :], AF.Exp, scale=-0.5)
        mmg(PB[0][:, :], [(onesr[0:1, 0:128], rnrow[0:1, :])])
        for ck in range(NCH):
            vtt(kfm[:, ck, :], kfm[:, ck, :], PB[0][:, :], OP.mult)

        for i_ in range(4):
            for j_ in range(i_ + 1):
                mmg(PB[2][:, 0:128],
                    [(kfm[:, ck, j_ * 128:(j_ + 1) * 128],
                      kfm[:, ck, i_ * 128:(i_ + 1) * 128]) for ck in range(NCH)])
                if j_ < i_:
                    act(gneg[:, pair_idx[(j_, i_)], :], PB[2][:, 0:128],
                        AF.Identity, scale=-1.0)
                else:
                    vtt(Lp[:, 0, :], PB[2][:, 0:128], maskL[:, :], OP.mult)
                    vtt(Rp[:, 0, :], PB[2][:, 0:128], maskU[:, :], OP.mult)
                    vtt(Pp[:, 0, :], eyef[:, :], Rp[:, 0, :], OP.subtract)
                    cl = cr = cp = 0
                    for mlev in range(1, 6):
                        mmg(PB[3][:, 0:128], [(Rp[:, cr, :], Lp[:, cl, :])])
                        mmg(PB[3][:, 128:256], [(Lp[:, cl, :], Rp[:, cr, :])])
                        act(Lp[:, 1 - cl, :], PB[3][:, 0:128], AF.Identity)
                        act(Rp[:, 1 - cr, :], PB[3][:, 128:256], AF.Identity)
                        mmg(PB[3][:, 256:384],
                            [(Lp[:, 1 - cl, :], Pp[:, cp, :]),
                             (eyef[:, :], Pp[:, cp, :])])
                        act(Pp[:, 1 - cp, :], PB[3][:, 256:384], AF.Identity)
                        cl, cr, cp = 1 - cl, 1 - cr, 1 - cp
                    act(Qb[:, i_, :], Pp[:, cp, :], AF.Identity)

            specs = [(ln1e[:, c, i_ * 128:(i_ + 1) * 128], wv[:, c, :])
                     for c in range(NCH)]
            specs.append((onesrb[0:1, 0:128], g["gbv_r"][0:1, :]))
            specs += [(kfm[:, ck, i_ * 128:(i_ + 1) * 128], Scur[:, ck, :])
                      for ck in range(NCH)]
            specs += [(gneg[:, pair_idx[(j_, i_)], :], u_tm[:, j_, :])
                      for j_ in range(i_)]
            mmg(PB[0][:, :], specs)
            vcopy(rhsb[:, :], PB[0][:, :])
            mmg(PB[1][:, :], [(Qb[:, i_, :], rhsb[:, :])])
            act(u_tm[:, i_, :], PB[1][:, :], AF.Identity)
            act(lru[:, i_, :], PB[1][:, :], AF.Identity,
                scale=lr_c[:, (s * 4 + i_):(s * 4 + i_) + 1])

        for ck in range(NCH):
            bank = PB[4] if ck % 2 == 0 else PB[5]
            specs = [(eyeb[:, :], Scur[:, ck, :])]
            specs += [(ktm[:, tc, ck * 128:(ck + 1) * 128], u_tm[:, tc, :])
                      for tc in range(4)]
            mmg(bank[:, :], specs)
            act(Snew[:, ck, :], bank[:, :], AF.Identity)
        for ck in range(NCH):
            bank = PB[4] if ck % 2 == 0 else PB[5]
            specs = [(eyeb[:, :], Mcur[:, ck, :])]
            specs += [(ktm[:, tc, ck * 128:(ck + 1) * 128], lru[:, tc, :])
                      for tc in range(4)]
            mmg(bank[:, :], specs)
            act(Mnew[:, ck, :], bank[:, :], AF.Identity)
        Scur, Snew = Snew, Scur
        Mcur, Mnew = Mnew, Mcur

        if s == 0:
            tap("ln1e0", ln1e[:, :, :], [128, NCH, SBW], BF16)
            tap("ktm0", ktm[:, :, :], [128, 4, C], BF16)
            tap("kfm0", kfm[:, :, :], [128, NCH, SBW], BF16)
            tap("u0", u_tm[:, :, :], [128, 4, R], BF16)
            tap("Q0", Qb[:, :, :], [128, 4, 128], BF16)
            tap("lr0", lr_c[:, 0:4], [128, 4], F32)
        if s == NSB - 1:
            tap("St", Scur[:, :, :], [128, NCH, R], BF16)
            tap("Mt", Mcur[:, :, :], [128, NCH, R], BF16)

    for ck in range(NCH):
        dma("sync", m_in_b.ap()[ck * 128:(ck + 1) * 128, :], Mcur[:, ck, :])
    ccg("AllGather", OP.bypass, [[0, 1], [2, 3], [4, 5], [6, 7]],
        m_in_b.ap(), m_out_b.ap())

    K.set_floor()

    wqs, qfm, h1e = g["wq_s"], g["qfm"], g["h1e"]
    tro0, tro1 = g["tro0"], g["tro1"]
    for c in range(NCH):
        dma("gpsimd", wqs[:, c, :], wq_h.ap()[c * 128:(c + 1) * 128, :])
    for t_ in range(NTO):
        xb = g["xb0"]
        for c in range(NCH):
            dma("gpsimd", xb[:, c, :],
                xo_fm.ap()[c * 128:(c + 1) * 128, t_ * 512:(t_ + 1) * 512])

        def xoc(c, xb=xb):
            return xb[:, c, :]

        sl = slice(t_ * 512, (t_ + 1) * 512)
        ln_stats(xoc, tr2[:, :], tr3[:, :], fused=True,
                 r1_out=tro0[:, :], mr1_out=tro1[:, :])
        mmg(PB[6][:, :], [(onesr[0:1, 0:128], tr2[0:1, :])])
        mmg(PB[7][:, :], [(onesr[0:1, 0:128], tr3[0:1, :])])
        mmg(PB[0][:, :], [(onesr[0:1, 0:128], tro0[0:1, :])])
        for c in range(NCH):
            vtt(stt[:, :], xoc(c), PB[6][:, :], OP.mult)
            vtt(stt2[:, :], stt[:, :], PB[7][:, :], OP.subtract)
            act(ln1e[:, c, :], stt2[:, :], AF.Identity,
                bias=g["dnb_c"][:, c:c + 1], scale=g["dng_c"][:, c:c + 1])
            vtt(h1e[:, c, sl], xoc(c), PB[0][:, :], OP.mult)
        for ck in range(NCH):
            bank = PB[1] if ck % 2 == 0 else PB[2]
            mmg(bank[:, :], [(wqs[:, c, ck * 128:(ck + 1) * 128], ln1e[:, c, :])
                             for c in range(NCH)])
            act(qfm[:, ck, sl], bank[:, :], AF.Identity,
                bias=g["gbq_c"][:, ck:ck + 1])
        vts1(g["mr1neg"][0:1, sl], tro1[0:1, :], -1.0, OP.mult)
    tap("qfmt", qfm[:, :, :], [128, NCH, TO], BF16)
    tap("h1et", h1e[:, :, :], [128, NCH, TO], BF16)

    K.set_floor()

    Mfull, yfm = g["Mfull"], g["yfm"]
    for ck in range(NCH):
        dma("sync", Mfull[:, ck, 0:512], m_out_b.ap()[ck * 128:(ck + 1) * 128, :])
        dma("sync", Mfull[:, ck, 512:1024],
            m_out_b.ap()[C + ck * 128:C + (ck + 1) * 128, :])
    tap("Mfull", Mfull[:, :, :], [128, NCH, C], BF16)
    for t_ in range(NTO):
        sl = slice(t_ * 512, (t_ + 1) * 512)
        for cv in range(NCH):
            bank = PB[3] if cv % 2 == 0 else PB[4]
            mmg(bank[:, :], [(Mfull[:, ck, cv * 128:(cv + 1) * 128], qfm[:, ck, sl])
                             for ck in range(NCH)])
            act(yfm[:, cv, sl], bank[:, :], AF.Identity)
    tap("yfm", yfm[:, :, :], [128, NCH, TO], BF16)

    K.set_floor()

    outw, h2 = g["outw_s"], g["h2_s"]
    for c in range(NCH):
        dma("gpsimd", outw[:, c, :], outw_h.ap()[c * 128:(c + 1) * 128, :])
    for t_ in range(NTO):
        sl = slice(t_ * 512, (t_ + 1) * 512)
        for co in range(NCH):
            bank = PB[0] if co % 2 == 0 else PB[1]
            specs = [(outw[:, cv, co * 128:(co + 1) * 128], yfm[:, cv, sl])
                     for cv in range(NCH)]
            specs.append((g["outb_r"][0:1, co * 128:(co + 1) * 128], onesrb[0:1, :]))
            specs.append((onesr[0:1, 0:128], g["mr1neg"][0:1, sl]))
            mmg(bank[:, :], specs)
            vtt(h2[:, co, sl], h1e[:, co, sl], bank[:, :], OP.add)
    tap("h2t", h2[:, :, :], [128, NCH, TO], BF16)

    K.set_floor()

    w1s, w2s, ln2e, mid, x0t, x1 = (g["w1_s"], g["w2_s"], g["ln2e"], g["mid_s"],
                                    g["x0t"], g["x1_s"])
    for c in range(NCH):
        dma("gpsimd", w1s[:, c, :], w1_h.ap()[c * 128:(c + 1) * 128, :])
        dma("gpsimd", w2s[:, c, :], w2_h.ap()[c * 128:(c + 1) * 128, :])
    for t_ in range(NTO):
        sl = slice(t_ * 512, (t_ + 1) * 512)

        def h2c(c, sl=sl):
            return h2[:, c, sl]

        ln_stats(h2c, tr2[:, :], tr3[:, :], fused=False)
        mmg(PB[6][:, :], [(onesr[0:1, 0:128], tr2[0:1, :])])
        mmg(PB[7][:, :], [(onesr[0:1, 0:128], tr3[0:1, :])])
        for c in range(NCH):
            vtt(stt[:, :], h2[:, c, sl], PB[6][:, :], OP.mult)
            vtt(stt2[:, :], stt[:, :], PB[7][:, :], OP.subtract)
            act(ln2e[:, c, :], stt2[:, :], AF.Identity,
                bias=g["mnb_c"][:, c:c + 1], scale=g["mng_c"][:, c:c + 1])
        for co in range(NCH):
            bank = PB[0] if co % 2 == 0 else PB[1]
            mmg(bank[:, :], [(w1s[:, c, co * 128:(co + 1) * 128], ln2e[:, c, :])
                             for c in range(NCH)])
            act(mid[:, co, :], bank[:, :], AF.Relu, bias=g["b1r_c"][:, co:co + 1])
        for c in range(NCH):
            dma("sync", x0t[:, c, :],
                xo_fm.ap()[c * 128:(c + 1) * 128, t_ * 512:(t_ + 1) * 512])
        for co in range(NCH):
            bank = PB[2] if co % 2 == 0 else PB[3]
            specs = [(w2s[:, c, co * 128:(co + 1) * 128], mid[:, c, :])
                     for c in range(NCH)]
            specs.append((g["b2r_r"][0:1, co * 128:(co + 1) * 128], onesrb[0:1, :]))
            mmg(bank[:, :], specs)
            vtt(stt[:, :], x0t[:, co, :], h2[:, co, sl], OP.add)
            vtt(x1[:, co, sl], stt[:, :], bank[:, :], OP.add)
    tap("x1t", x1[:, :, :], [128, NCH, TO], BF16)

    K.set_floor()

    g1t, xqs = g["g1t"], g["xq_s"]
    for t_ in range(NTO):
        sl = slice(t_ * 512, (t_ + 1) * 512)

        def x1c(c, sl=sl):
            return x1[:, c, sl]

        ln_stats(x1c, tr2[:, :], tr3[:, :], fused=False)
        mmg(PB[6][:, :], [(onesr[0:1, 0:128], tr2[0:1, :])])
        mmg(PB[7][:, :], [(onesr[0:1, 0:128], tr3[0:1, :])])
        for c in range(NCH):
            vtt(stt[:, :], x1[:, c, sl], PB[6][:, :], OP.mult)
            vtt(g1t[:, c, sl], stt[:, :], PB[7][:, :], OP.subtract)
            vred(g["gmaxa"][:, t_ * NCH + c:t_ * NCH + c + 1], g1t[:, c, sl],
                 OP.max, absval=True)
    tap("g1tt", g1t[:, :, :], [128, NCH, TO], BF16)

    def partition_tree_max():
        h = 64
        while h >= 1:
            dma("sync", g["red1"][0:h, 0:1], g["red0"][h:2 * h, 0:1])
            vtt(g["red0"][0:h, 0:1], g["red0"][0:h, 0:1], g["red1"][0:h, 0:1],
                OP.max)
            h //= 2

    vred(g["red0"][:, :], g["gmaxa"][:, :], OP.max)
    partition_tree_max()
    mmg(PB[2][0:1, 0:16], [(g["red0"][0:1, 0:1], onesr[0:1, 0:16])])
    act(g["sval"][:, :], PB[2][0:1, 0:16], AF.Identity)
    dma("sync", gmax_in.ap(), g["sval"][:, :])
    wsc, wabs2 = g["wsc"], [g["wabs"], g["wabs2"]]
    wstr2 = [g["wstr"], g["wstr2"]]
    invrow = g["invrow"]

    def wq_pass1(src_h, nchunks, width, wrow):
        for c in range(nchunks):
            wstr = wstr2[c % 2]
            wabs = wabs2[c % 2]
            dma("gpsimd", wstr[:, 0:width], src_h.ap()[c * 128:(c + 1) * 128, :])
            act(wabs[:, 0:width], wstr[:, 0:width], AF.Abs)
            if c == 0:
                vcopy(wsc[:, 0:width], wabs[:, 0:width])
            else:
                vtt(wsc[:, 0:width], wsc[:, 0:width], wabs[:, 0:width], OP.max)
        h = 64
        while h >= 1:
            for w0 in range(0, width, 512):
                w1_ = min(width, w0 + 512)
                dma("sync", stt2[0:h, 0:w1_ - w0], wsc[h:2 * h, w0:w1_])
                vtt(wsc[0:h, w0:w1_], wsc[0:h, w0:w1_], stt2[0:h, 0:w1_ - w0],
                    OP.max)
            h //= 2
        vcopy(wrow[0:1, 0:width], wsc[0:1, 0:width])

    def wq_pass2(src_h, spill_h, nchunks, width, wrow):
        act(invrow[0:1, 0:width], wrow[0:1, 0:width], AF.Ln, scale=1.0 / 127.0)
        act(invrow[0:1, 0:width], invrow[0:1, 0:width], AF.Exp, scale=-1.0)
        for c in range(nchunks):
            wstr = wstr2[c % 2]
            dma("gpsimd", wstr[:, 0:width], src_h.ap()[c * 128:(c + 1) * 128, :])
            for w0 in range(0, width, 512):
                w1_ = min(width, w0 + 512)
                mmg(PB[2][:, 0:w1_ - w0],
                    [(onesr[0:1, 0:128], invrow[0:1, w0:w1_])])
                vtt(stt[:, 0:w1_ - w0], wstr[:, w0:w1_], PB[2][:, 0:w1_ - w0],
                    OP.mult)
                vts2(wstr[:, w0:w1_], stt[:, 0:w1_ - w0], RND, RND,
                     OP.add, OP.subtract)
            dma("sync", spill_h.ap()[c * 128:(c + 1) * 128, :], wstr[:, 0:width])

    wq_pass1(wg_h, NCH, I, g["wrow_g"])
    wq_pass1(wu_h, NCH, I, g["wrow_u"])
    wq_pass1(wd_h, ICH, C, g["wrow_d"])

    ccg("AllGather", OP.bypass, [[0, 1, 2, 3, 4, 5, 6, 7]],
        gmax_in.ap(), gmax_out.ap())

    wq_pass2(wg_h, wgq_sp, NCH, I, g["wrow_g"])
    wq_pass2(wu_h, wuq_sp, NCH, I, g["wrow_u"])
    wq_pass2(wd_h, wdq_sp, ICH, C, g["wrow_d"])
    for (wr, colt, width) in ((g["wrow_g"], g["sgi_c"], I),
                              (g["wrow_u"], g["sui_c"], I),
                              (g["wrow_d"], g["sdh_c"], C)):
        vts1(invrow[0:1, 0:width], wr[0:1, :], 1.0 / 127.0, OP.mult)
        dma("sync", row_sp.ap()[0:1, 0:width], invrow[0:1, 0:width])
        dma("sync", colt[:, :],
            row_sp.ap()[0, 0:width].rearrange("(a b) -> b a", b=128), slow=True)

    dma("sync", g["mx128"][:, :], gmax_out.ap().rearrange("a b -> (a b)").unsqueeze(0))
    vred(g["sval"][0:1, 0:1], g["mx128"][:, :], OP.max)
    act(g["sval"][0:1, 1:2], g["clipx_r"][0:1, 0:1], AF.Sigmoid)
    vtt(g["sval"][0:1, 2:3], g["sval"][0:1, 0:1], g["sval"][0:1, 1:2], OP.mult)
    vts1(g["sval"][0:1, 2:3], g["sval"][0:1, 2:3], 1.0 / 127.0, OP.mult)
    vrec(g["sval"][0:1, 3:4], g["sval"][0:1, 2:3])
    mmg(PB[2][:, 0:1], [(onesr[0:1, 0:128], g["sval"][0:1, 3:4])])
    act(g["sxc"][:, :], PB[2][:, 0:1], AF.Identity)
    mmg(PB[2][:, 0:1], [(onesr[0:1, 0:128], g["sval"][0:1, 2:3])])
    act(g["sxv"][:, :], PB[2][:, 0:1], AF.Identity)
    for t_ in range(NTO):
        sl = slice(t_ * 512, (t_ + 1) * 512)
        for c in range(NCH):
            vts1(stt[:, :], g1t[:, c, sl], g["sxc"][:, 0:1], OP.mult)
            vts2(stt[:, :], stt[:, :], RND, RND, OP.add, OP.subtract)
            vts2(xqs[:, c, sl], stt[:, :], -128.0, 127.0, OP.max, OP.min)
    tap("xqt", xqs[:, :, :], [128, NCH, TO], BF16)
    vts1(g["sgi_c"][:, :], g["sgi_c"][:, :], g["sxv"][:, 0:1], OP.mult)
    vts1(g["sui_c"][:, :], g["sui_c"][:, :], g["sxv"][:, 0:1], OP.mult)

    K.set_floor()

    wgt = [g["wgt0"], g["wgt1"]]
    wut = [g["wut0"], g["wut1"]]
    h_t, silu_t, up_t = g["h_t"], g["silu_t"], g["up_t"]
    for ic in range(ICH):
        bw = wgt[ic % 2]
        bu = wut[ic % 2]
        for hh in range(NCH):
            dma("sync", bw[:, hh, :],
                wgq_sp.ap()[hh * 128:(hh + 1) * 128, ic * 128:(ic + 1) * 128])
            dma("sync", bu[:, hh, :],
                wuq_sp.ap()[hh * 128:(hh + 1) * 128, ic * 128:(ic + 1) * 128])
        for t_ in range(NTO):
            sl = slice(t_ * 512, (t_ + 1) * 512)
            bg = PB[0] if t_ % 2 == 0 else PB[1]
            bu_ = PB[2] if t_ % 2 == 0 else PB[3]
            mmg(bg[:, :], [(bw[:, hh, :], xqs[:, hh, sl]) for hh in range(NCH)])
            mmg(bu_[:, :], [(bu[:, hh, :], xqs[:, hh, sl]) for hh in range(NCH)])
            act(silu_t[:, :], bg[:, :], AF.Silu, scale=g["sgi_c"][:, ic:ic + 1])
            act(up_t[:, :], bu_[:, :], AF.Identity, scale=g["sui_c"][:, ic:ic + 1])
            vtt(h_t[:, :], silu_t[:, :], up_t[:, :], OP.mult)
            vred(g["habs"][:, t_ * ICH + ic:t_ * ICH + ic + 1], h_t[:, :],
                 OP.max, absval=True)
            dma("sync", h_spill.ap()[ic, :, t_ * 512:(t_ + 1) * 512], h_t[:, :])

    vred(g["red0"][:, :], g["habs"][:, :], OP.max)
    partition_tree_max()
    mmg(PB[2][0:1, 0:16], [(g["red0"][0:1, 0:1], onesr[0:1, 0:16])])
    act(g["sval"][:, :], PB[2][0:1, 0:16], AF.Identity)
    dma("sync", hmax_in.ap(), g["sval"][:, :])
    ccg("AllGather", OP.bypass, [[0, 1, 2, 3, 4, 5, 6, 7]],
        hmax_in.ap(), hmax_out.ap())
    dma("sync", g["mx128"][:, :], hmax_out.ap().rearrange("a b -> (a b)").unsqueeze(0))
    vred(g["sval"][0:1, 0:1], g["mx128"][:, :], OP.max)
    act(g["sval"][0:1, 1:2], g["clipd_r"][0:1, 0:1], AF.Sigmoid)
    vtt(g["sval"][0:1, 2:3], g["sval"][0:1, 0:1], g["sval"][0:1, 1:2], OP.mult)
    vts1(g["sval"][0:1, 2:3], g["sval"][0:1, 2:3], 1.0 / 127.0, OP.mult)
    vrec(g["sval"][0:1, 3:4], g["sval"][0:1, 2:3])
    mmg(PB[2][:, 0:1], [(onesr[0:1, 0:128], g["sval"][0:1, 3:4])])
    act(g["shc"][:, :], PB[2][:, 0:1], AF.Identity)
    mmg(PB[2][:, 0:1], [(onesr[0:1, 0:128], g["sval"][0:1, 2:3])])
    act(g["shv"][:, :], PB[2][:, 0:1], AF.Identity)
    vts1(g["sdh_c"][:, :], g["sdh_c"][:, :], g["shv"][:, 0:1], OP.mult)

    K.set_floor()

    wdt = [g["wdt0"], g["wdt1"]]
    hq_t, outt = g["hq_t"], g["outt"]
    for t_ in range(NTO):
        sl = slice(t_ * 512, (t_ + 1) * 512)
        for ic in range(ICH):
            wb = wdt[ic % 2]
            dma("sync", wb[:, :], wdq_sp.ap()[ic * 128:(ic + 1) * 128, :])
            dma("sync", h_t[:, :], h_spill.ap()[ic, :, t_ * 512:(t_ + 1) * 512])
            vts1(stt[:, :], h_t[:, :], g["shc"][:, 0:1], OP.mult)
            vts2(stt[:, :], stt[:, :], RND, RND, OP.add, OP.subtract)
            vts2(hq_t[:, :], stt[:, :], -128.0, 127.0, OP.max, OP.min)
            for ho in range(NCH):
                mma(PB[ho][:, :], (wb[:, ho * 128:(ho + 1) * 128], hq_t[:, :]),
                    ic == 0, ic == ICH - 1)
        for ho in range(NCH):
            act(outt[:, :], PB[ho][:, :], AF.Identity, scale=g["sdh_c"][:, ho:ho + 1])
            vtt(outt[:, :], outt[:, :], x1[:, ho, sl], OP.add)
            dma("sync", out_h.ap()[ho * 128:(ho + 1) * 128, t_ * 512:(t_ + 1) * 512],
                outt[:, :])

    with nc.Block() as block:
        K.finalize(block)
    return nc, taps_out


def kernel(L0_attn_x, L0_attn_dn_g, L0_attn_dn_b, L0_attn_gates_w, L0_attn_gates_b,
           L0_attn_lr_w, L0_attn_lr_b, L0_attn_out_w, L0_attn_out_b, L0_attn_mn_g,
           L0_attn_mn_b, L0_attn_w1, L0_attn_b1, L0_attn_w2, L0_attn_b2,
           L0_ffn_clip_x, L0_ffn_clip_down, L0_ffn_w_gate, L0_ffn_w_up,
           L0_ffn_w_down):
    x = np.asarray(L0_attn_x, np.float32)
    gw = np.ascontiguousarray(np.asarray(L0_attn_gates_w, np.float32))
    gb = np.asarray(L0_attn_gates_b, np.float32)
    shared = {
        "wq": np.ascontiguousarray(gw[:, 0:C]),
        "wk": np.ascontiguousarray(gw[:, C:2 * C]),
        "lrw": np.ascontiguousarray(np.asarray(L0_attn_lr_w, np.float32)),
        "outw": np.ascontiguousarray(np.asarray(L0_attn_out_w, np.float32)),
        "w1": np.ascontiguousarray(np.asarray(L0_attn_w1, np.float32)),
        "w2": np.ascontiguousarray(np.asarray(L0_attn_w2, np.float32)),
        "wgT": np.ascontiguousarray(np.asarray(L0_ffn_w_gate, np.float32).T),
        "wuT": np.ascontiguousarray(np.asarray(L0_ffn_w_up, np.float32).T),
        "wdT": np.ascontiguousarray(np.asarray(L0_ffn_w_down, np.float32).T),
        "gbq": gb[0:C].reshape(1, C),
        "gbk": gb[C:2 * C].reshape(1, C),
        "outb": np.asarray(L0_attn_out_b, np.float32).reshape(1, C),
        "b1r": np.asarray(L0_attn_b1, np.float32).reshape(1, C),
        "b2r": np.asarray(L0_attn_b2, np.float32).reshape(1, C),
        "dng": np.asarray(L0_attn_dn_g, np.float32).reshape(1, C),
        "dnb": np.asarray(L0_attn_dn_b, np.float32).reshape(1, C),
        "mng": np.asarray(L0_attn_mn_g, np.float32).reshape(1, C),
        "mnb": np.asarray(L0_attn_mn_b, np.float32).reshape(1, C),
        "lrb": np.asarray(L0_attn_lr_b, np.float32).reshape(1, 1),
        "clipx": np.asarray(L0_ffn_clip_x, np.float32).reshape(1, 1),
        "clipd": np.asarray(L0_ffn_clip_down, np.float32).reshape(1, 1),
    }
    shared = {k: np.ascontiguousarray(v) for k, v in shared.items()}

    if "nc" not in _CACHE:
        _CACHE["nc"] = build_nc()
    nc, tap_names = _CACHE["nc"]

    in_maps = []
    for core in range(8):
        b, hh = core // 2, core % 2
        xb_fm = np.ascontiguousarray(x[b].T)
        xo = np.ascontiguousarray(xb_fm[:, hh * TO:(hh + 1) * TO])
        m = dict(shared)
        m["x_fm"] = xb_fm
        m["xo_fm"] = xo
        m["wv"] = np.ascontiguousarray(gw[:, 2 * C + hh * R: 2 * C + (hh + 1) * R])
        m["gbv"] = np.ascontiguousarray(
            gb[2 * C + hh * R: 2 * C + (hh + 1) * R].reshape(1, R))
        in_maps.append(m)

    res = run_bass_kernel_spmd(nc, in_maps, core_ids=list(range(8)))
    out = np.empty((N, T, C), np.float32)
    for core in range(8):
        b, hh = core // 2, core % 2
        out[b, hh * TO:(hh + 1) * TO, :] = res.results[core]["out_fm"].T
    if TAPS:
        kernel.taps = [{n: r["tap_" + n] for n in tap_names} for r in res.results]
    return out
